# version 03
# baseline (speedup 1.0000x reference)
import numpy as np
import ml_dtypes
import concourse.bass as bass
import concourse.mybir as mybir
from concourse.bass_utils import run_bass_kernel_spmd

F32 = mybir.dt.float32
BF16 = mybir.dt.bfloat16
AF = mybir.ActivationFunctionType
ALU = mybir.AluOpType
AX = mybir.AxisListType.X

NCORES = 8
S = 8192
D = 2048
TOK = 1024
HALO = 256
KT = TOK + HALO
NT = 8
HEADS = 16
HD = 128
HID = 8192
NHC = 64
NDC = 16
BAND = 384
BIGNEG = -float(2.0 ** 100)
SCALE_ATTN = float(HD) ** -0.5
EPS1 = 1e-5
EPS2 = 1e-8
THALF = 512
NT_H = 4
NTOT = float(S * D)

DBG = []


class Sched:
    ENGS = ("t", "v", "se", "g", "s")

    def __init__(self, nc, sems, cnt):
        self.nc = nc
        self.sems = sems
        self.cnt = cnt
        self.steps = []

    def step(self, eng, deps, fn, guard=False, noinc=False, sem=None):
        assert eng in self.ENGS
        if eng in ("g", "s") and not noinc:
            assert sem is not None, "DMA steps need a slot sem"
        else:
            sem = eng
        if sem not in self.sems:
            self.sems[sem] = self.nc.alloc_semaphore("sem_" + sem)
            self.cnt[sem] = 0
        for d in deps:
            assert 0 <= d < len(self.steps), (d, len(self.steps))
        self.steps.append([eng, list(deps), fn, guard, noinc, sem])
        return len(self.steps) - 1

    def emit(self):
        n = len(self.steps)
        synced = [False] * n
        for eng, deps, fn, guard, noinc, sem in self.steps:
            for d in deps:
                if self.steps[d][0] != eng or self.steps[d][3] or guard:
                    synced[d] = True
        for i, st in enumerate(self.steps):
            if st[0] in ("g", "s") and not st[4]:
                synced[i] = True
            if st[4]:
                assert not synced[i], ("noinc step has dependents", i)
        post = [0] * n
        c = dict(self.cnt)
        for i, st in enumerate(self.steps):
            if synced[i]:
                c[st[5]] += 16 if st[0] in ("g", "s") else 1
            post[i] = c[st[5]]
        per_eng = {e: [] for e in self.ENGS}
        for i, st in enumerate(self.steps):
            per_eng[st[0]].append(i)
        sems, steps = self.sems, self.steps

        dbg_mark = self.cnt.get("dbg")

        def mk(ek):
            ids = per_eng[ek]

            def body(eng):
                if dbg_mark and ek in ("t", "v", "se"):
                    eng.wait_ge(sems["dbg"], dbg_mark)
                for i in ids:
                    _, deps, fn, guard, _noinc, _sem = steps[i]
                    for d in deps:
                        dk = steps[d][0]
                        if dk != ek or steps[d][3] or guard:
                            assert synced[d], (i, d)
                            eng.wait_ge(sems[steps[d][5]], post[d])
                    last = fn(eng)
                    if synced[i]:
                        assert last is not None, ("step needs inc", i)
                        last.then_inc(sems[steps[i][5]],
                                      16 if ek in ("g", "s") else 1)
            return body

        with self.nc.Block() as blk:
            if per_eng["t"]:
                blk.tensor(mk("t"))
            if per_eng["v"]:
                blk.vector(mk("v"))
            if per_eng["se"]:
                blk.scalar(mk("se"))
            if per_eng["g"]:
                blk.gpsimd(mk("g"))
            if per_eng["s"]:
                blk.sync(mk("s"))
        self.cnt.update(c)


def build_program():
    import os
    KPH = int(os.environ.get("KPH", "9"))
    nc = bass.Bass("TRN2", target_bir_lowering=False)

    dX = nc.dram_tensor("x_own", [TOK, D], F32, kind="ExternalInput").ap()
    dHAL = nc.dram_tensor("x_halo", [HALO, D], F32, kind="ExternalInput").ap()
    dXF = nc.dram_tensor("x_full", [S, D], F32, kind="ExternalInput").ap()
    dWQKV = nc.dram_tensor("wqkv_s", [48, NDC, 128, 128], F32, kind="ExternalInput").ap()
    dWOUT = nc.dram_tensor("wout_s", [16, 16, 128, 128], F32, kind="ExternalInput").ap()
    dWC1 = nc.dram_tensor("wc1_s", [4, NDC, 128, 128], F32, kind="ExternalInput").ap()
    dWC2 = nc.dram_tensor("wc2_s", [4, 128, 1], F32, kind="ExternalInput").ap()
    dWG = nc.dram_tensor("wg_s", [NHC, NDC, 128, 128], F32, kind="ExternalInput").ap()
    dWU = nc.dram_tensor("wu_s", [NHC, NDC, 128, 128], F32, kind="ExternalInput").ap()
    dWD = nc.dram_tensor("wd_s", [16, NHC, 128, 128], F32, kind="ExternalInput").ap()
    dCOS = nc.dram_tensor("cosk", [HD, KT], F32, kind="ExternalInput").ap()
    dSIN = nc.dram_tensor("sink", [HD, KT], F32, kind="ExternalInput").ap()
    dCOSL = nc.dram_tensor("coskl", [HD, KT], F32, kind="ExternalInput").ap()
    dSINL = nc.dram_tensor("sinkl", [HD, KT], F32, kind="ExternalInput").ap()
    dCOSQ = nc.dram_tensor("cosq", [HD, TOK], F32, kind="ExternalInput").ap()
    dSINQ = nc.dram_tensor("sinq", [HD, TOK], F32, kind="ExternalInput").ap()
    dCOSQL = nc.dram_tensor("cosql", [HD, TOK], F32, kind="ExternalInput").ap()
    dSINQL = nc.dram_tensor("sinql", [HD, TOK], F32, kind="ExternalInput").ap()
    dRELP = nc.dram_tensor("relp1", [128, BAND], F32, kind="ExternalInput").ap()
    dMREL = nc.dram_tensor("mrel0", [128, BAND], F32, kind="ExternalInput").ap()
    dBND0 = nc.dram_tensor("bnd0", [128, BAND], F32, kind="ExternalInput").ap()
    dBND1 = nc.dram_tensor("bnd1", [128, BAND], F32, kind="ExternalInput").ap()
    dOUT = nc.dram_tensor("out", [TOK, D], F32, kind="ExternalOutput").ap()

    dSGG = nc.dram_tensor("sgg", [NHC, NDC * 128 * 128], BF16, kind="Internal").ap()
    dSGU = nc.dram_tensor("sgu", [NHC, NDC * 128 * 128], BF16, kind="Internal").ap()
    dSGD = nc.dram_tensor("sgd", [16, NHC * 128 * 128], BF16, kind="Internal").ap()

    identity_bf = nc.inline_tensor(
        np.eye(128, dtype=np.float32).astype(ml_dtypes.bfloat16), name="ident_c").ap()
    ones_row_bf = nc.inline_tensor(
        np.ones((1, 128), dtype=np.float32).astype(ml_dtypes.bfloat16), name="onesr_c").ap()
    ones_col_bf = nc.inline_tensor(
        np.ones((128, 1), dtype=np.float32).astype(ml_dtypes.bfloat16), name="onesc_c").ap()

    dbg = {}
    for nm in DBG:
        shp, dt_ = {
            "d_w": ([1, 16], F32), "d_colsum": ([1, D], F32),
            "d_ln1t": ([128, NDC * KT], BF16), "d_attnout": ([128, 16384], BF16),
            "d_x2": ([TOK, D], F32), "d_xqt": ([128, NDC * THALF], BF16),
            "d_invs": ([128, 8], F32),
        }[nm]
        dbg[nm] = nc.dram_tensor(nm, shp, dt_, kind="ExternalOutput").ap()

    SB0 = 16512

    def A(name, shape, dt_, offset):
        return nc.alloc_sbuf_tensor_at(name, shape, dt_, offset=offset + SB0)
    ln1T = A("ln1T", [128, NDC * KT], BF16, offset=0)
    cosk = A("cosk", [128, KT], BF16, offset=40960)
    sink = A("sink", [128, KT], BF16, offset=43520)
    coskl = A("coskl", [128, KT], BF16, offset=46080)
    sinkl = A("sinkl", [128, KT], BF16, offset=48640)
    cosq = A("cosq", [128, TOK], BF16, offset=51200)
    sinq = A("sinq", [128, TOK], BF16, offset=53248)
    cosql = A("cosql", [128, TOK], BF16, offset=55296)
    sinql = A("sinql", [128, TOK], BF16, offset=57344)
    x2res = A("x2res", [128, NT * D], F32, offset=0)
    wc1b = A("wc1b", [128, 4 * D], BF16, offset=65536)
    xmrow = A("xmrow", [1, D], F32, offset=65536 + 16384)
    xmrow_b = A("xmrow_b", [1, D], BF16, offset=65536 + 24576)
    xmT = A("xmT", [128, 16], BF16, offset=65536 + 28672)
    hTb = A("hTb", [128, 4], BF16, offset=65536 + 28704)
    off = 65536
    qbb = A("qbb", [128, TOK], BF16, offset=off); off += 2048
    qsh = A("qsh", [128, TOK], BF16, offset=off); off += 2048
    qrp = A("qrp", [128, TOK], BF16, offset=off); off += 2048
    kbb = A("kbb", [128, KT], BF16, offset=off); off += 2560
    ksh = A("ksh", [128, KT], BF16, offset=off); off += 2560
    krp = A("krp", [128, KT], BF16, offset=off); off += 2560
    vbf = A("vbf", [128, KT], BF16, offset=off); off += 2560
    vtb = A("vtb", [128, KT], BF16, offset=off); off += 2560
    qtm = A("qtm", [128, KT], BF16, offset=off); off += 2560
    pe = A("pe", [128, 8 * BAND], BF16, offset=off); off += 6144
    ptb = A("ptb", [128, 8 * BAND], BF16, offset=off); off += 6144
    wsl = []
    for i in range(2):
        trio = []
        for nmW in ("wqs", "wks", "wvs"):
            trio.append(A(f"{nmW}{i}", [128, D], BF16, offset=off)); off += 4096
        wsl.append(trio)
    assert off <= 131072, off
    opj = A("opj", [128, 16 * TOK], BF16, offset=65536)
    pprim = A("pprim", [128, NHC * THALF], BF16, offset=65536)
    attnout = A("attnout", [128, HEADS * TOK], BF16, offset=131072)
    xqT = A("xqT", [128, NDC * THALF], BF16, offset=131072)
    cst1 = A("cst1", [128, D], F32, offset=131072)
    ffnout = A("ffnout", [128, 16 * THALF], BF16, offset=131072 + 16384)
    sqscrC1 = A("sqscrC1", [128, D], BF16, offset=131072 + 16384)
    r3 = 163840
    xs0 = A("xs0", [128, D], BF16, offset=r3)
    xs1 = A("xs1", [128, D], BF16, offset=r3 + 4096)
    xsq = A("xsq", [128, D], BF16, offset=r3 + 8192)
    relp = A("relp", [128, BAND], F32, offset=r3 + 12288)
    mrel = A("mrel", [128, BAND], F32, offset=r3 + 13824)
    bnd0 = A("bnd0s", [128, BAND], F32, offset=r3 + 15360)
    bnd1 = A("bnd1s", [128, BAND], F32, offset=r3 + 16896)
    wmf = A("wmf", [128, BAND], F32, offset=r3 + 18432)
    s1buf = A("s1buf", [128, 64], F32, offset=r3 + 20480)
    s2buf = A("s2buf", [128, 64], F32, offset=r3 + 20736)
    tmp64a = A("tmp64a", [128, 64], F32, offset=r3 + 20992)
    tmp64b = A("tmp64b", [128, 64], F32, offset=r3 + 21248)
    irbbuf = A("irbbuf", [128, 64], BF16, offset=r3 + 21504)
    sqscrA = A("sqscrA", [128, D], BF16, offset=r3 + 20480)
    sqscrB = A("sqscrB", [128, D], BF16, offset=r3 + 16384)
    sqscrC5 = A("sqscrC5", [128, D], BF16, offset=r3 + 16384)
    xin0 = A("xin0", [128, D], F32, offset=r3)
    xin1 = A("xin1", [128, D], F32, offset=r3 + 8192)
    ln1st = A("ln1st", [128, D], BF16, offset=r3 + 16384)
    bstage = A("bstage", [128, D], F32, offset=r3 + 20480)
    wgl = [A(f"wgl{i}", [128, D], BF16, offset=r3 + 4096 * i) for i in range(2)]
    wul = [A(f"wul{i}", [128, D], BF16, offset=r3 + 8192 + 4096 * i) for i in range(2)]
    wdl = [A(f"wdl{i}", [128, D], BF16, offset=r3 + 16384 + 4096 * i) for i in range(2)]
    qtb = A("qtb", [128, 1024], F32, offset=r3 + 16384)
    kbf = A("kbf", [128, D], BF16, offset=r3 + 20480)
    ustg = A("ustg", [128, D], F32, offset=r3 + 16384)
    qu = A("qu", [128, D], F32, offset=r3 + 24576)
    cst0 = A("cst0", [128, D], F32, offset=r3 + 24576)
    qrr = A("qrr", [128, 1024], F32, offset=r3 + 32768)
    gsc = A("gsc", [128, THALF], F32, offset=r3 + 32768)
    silb = A("silb", [128, THALF], BF16, offset=r3 + 34816)
    r4 = 200704
    wm_g = A("wm_g", [128, BAND], BF16, offset=r4)
    wm_0 = A("wm_0", [128, BAND], BF16, offset=r4 + 768)
    wm_1 = A("wm_1", [128, BAND], BF16, offset=r4 + 1536)
    identsb = A("identsb", [128, 128], BF16, offset=r4 + 2304)
    onesrow = A("onesrow", [1, 128], BF16, offset=r4 + 2560)
    onescol = A("onescol", [128, 1], BF16, offset=r4 + 2816)
    sm = A("sm", [128, 48], F32, offset=r4 + 2848)
    invs_cols = A("invs_cols", [128, NT], F32, offset=r4 + 3040)
    invs_colsb = A("invs_colsb", [128, NT], BF16, offset=r4 + 3072)
    rmaxn = A("rmaxn", [128, 8], F32, offset=r4 + 3104)
    rmaxs = A("rmaxs", [128, 8], F32, offset=r4 + 3136)
    rsum = A("rsum", [128, 8], F32, offset=r4 + 3168)
    rinv = A("rinv", [128, 8], F32, offset=r4 + 3200)
    bn6 = A("bn6", [128, 6], F32, offset=r4 + 3232)
    scl1 = A("scl1", [1, 48], F32, offset=r4 + 3264)
    sclb = A("sclb", [1, 8], BF16, offset=r4 + 3456)
    wc2b = A("wc2b", [128, 4], BF16, offset=r4 + 3488)
    invsb = A("invsb", [128, THALF], F32, offset=r4 + 3520)
    srowb = A("srowb", [1, THALF], BF16, offset=r4 + 5568)
    ccol = A("ccol", [128, 8], F32, offset=r4 + 6592)
    assert r4 + 6592 + 32 <= 212000

    def smc(i):
        return sm[:, i:i + 1]

    def sc(i):
        return scl1[0:1, i:i + 1]

    PS = nc.alloc_psum_tensor("ps", [128, 4096], F32)

    sems = {k: nc.alloc_semaphore("sem_" + k) for k in ("t", "v", "se")}
    cnt = {e: 0 for e in ("t", "v", "se")}

    def sched():
        return Sched(nc, sems, cnt)

    def cc(i):
        return ccol[:, i:i + 1]

    stats_chain = {}

    def stats_steps(s_, src, scratch, deps):
        key = id(s_)
        prev = stats_chain.get(key)
        deps1 = list(deps) + ([prev[0]] if prev else [])
        st1 = s_.step("v", deps1, (lambda src=src: lambda v:
                      v.reduce_sum(smc(20), src, AX))(), guard=True)
        deps2 = list(deps) + [st1] + ([prev[1]] if prev else [])
        st2 = s_.step("se", deps2, (lambda src=src, scratch=scratch: lambda se:
                      se.activation(scratch, src, AF.Square,
                                    accum_out=smc(21)))(), guard=True)
        stats_chain[key] = (st1, st2)
        sm_ = s_.step("v", [st1], lambda v: v.tensor_scalar(
            smc(4), smc(20), 1.0 / D, None, ALU.mult), guard=True)
        se2 = s_.step("v", [st2], lambda v: v.tensor_scalar(
            smc(22), smc(21), 1.0 / D, None, ALU.mult), guard=True)
        sq_ = s_.step("v", [sm_], lambda v: v.scalar_tensor_tensor(
            smc(23), smc(4), -1.0, smc(4), ALU.mult, ALU.mult), guard=True)
        sv_ = s_.step("v", [se2, sq_], lambda v: v.tensor_add(
            smc(6), smc(22), smc(23)), guard=True)
        return sm_, sv_

    def cc1(i):
        return ccol[0:1, i:i + 1]

    s_ = sched()

    def f_consts(v):
        v.memset(scl1[:], 0.0)
        v.memset(invs_cols[:], 1.0)
        v.memset(ccol[:, 0:1], EPS1)
        v.memset(ccol[:, 1:2], EPS2)
        v.memset(ccol[:, 2:3], -0.5)
        v.memset(ccol[:, 3:4], -5.0)
        v.memset(ccol[:, 4:5], 10.0)
        return v.memset(ccol[:, 5:6], SCALE_ATTN)
    s_.step("v", [], f_consts, guard=True)
    ve_tile = [None] * 64
    for i in range(64):
        xs = xs0 if i % 2 == 0 else xs1
        deps = [ve_tile[i - 2]] if i >= 2 else []
        ld = s_.step("g", deps, (lambda xs=xs, i=i: lambda g:
                     g.dma_start(out=xs[:], in_=dXF[128 * i:128 * i + 128, :]))(),
                     sem=f"xs{i % 2}")

        def mk_ve(xs=xs, i=i):
            def f(v):
                v.reduce_sum(s1buf[:, i:i + 1], xs[:], AX)
                return v.scalar_tensor_tensor(xsq[:], xs[:], 1.0, xs[:],
                                              ALU.mult, ALU.mult,
                                              accum_out=s2buf[:, i:i + 1])
            return f
        vdeps = [ld] + ([ve_tile[i - 1]] if i >= 1 else [])
        ve_tile[i] = s_.step("v", vdeps, mk_ve(), guard=True)
    s_.step("g", [], lambda g: g.dma_start(out=cosk[:], in_=dCOS), sem="cg")
    s_.step("g", [], lambda g: g.dma_start(out=sink[:], in_=dSIN), sem="cg")
    s_.step("g", [], lambda g: g.dma_start(out=coskl[:], in_=dCOSL), sem="cg")
    s_.step("g", [], lambda g: g.dma_start(out=sinkl[:], in_=dSINL), sem="cg")
    s_.step("g", [], lambda g: g.dma_start(out=cosq[:], in_=dCOSQ), sem="cg")
    s_.step("g", [], lambda g: g.dma_start(out=sinq[:], in_=dSINQ), sem="cg")
    s_.step("g", [], lambda g: g.dma_start(out=cosql[:], in_=dCOSQL), sem="cg")
    s_.step("g", [], lambda g: g.dma_start(out=sinql[:], in_=dSINQL), sem="cg")
    s_.step("g", [], lambda g: g.dma_start(
        out=wc1b[:].rearrange("p (h d b) -> p h d b", h=4, d=16),
        in_=dWC1.rearrange("h d a b -> a h d b")), sem="cg")
    s_.step("g", [], lambda g: g.dma_start(
        out=wc2b[:].rearrange("p (h b) -> p h b", h=4),
        in_=dWC2.rearrange("h a b -> a h b"),
        allow_slow_non_contiguous=True), sem="cg")
    s_.step("g", [], lambda g: g.dma_start(out=identsb[:], in_=identity_bf), sem="cg")
    s_.step("g", [], lambda g: g.dma_start(out=onesrow[:], in_=ones_row_bf), sem="cg")
    s_.step("g", [], lambda g: g.dma_start(out=onescol[:], in_=ones_col_bf), sem="cg")
    for dsrc, dst in ((dRELP, relp), (dMREL, mrel), (dBND0, bnd0), (dBND1, bnd1)):
        s_.step("s", [], (lambda dsrc=dsrc, dst=dst: lambda s:
                s.dma_start(out=dst[:], in_=dsrc))(), sem="cs")
    s_.emit()
    if KPH <= 1:
        return nc
    CG_TOTAL = cnt["cg"]
    CS_TOTAL = cnt["cs"]

    s_ = sched()

    def f_cwait(eng):
        eng.wait_ge(sems["cg"], CG_TOTAL)
        eng.wait_ge(sems["cs"], CS_TOTAL)
        return None
    for _e in ("t", "v", "se"):
        s_.step(_e, [], f_cwait, noinc=True)
    a_m = s_.step("v", [], lambda v: v.tensor_scalar(
        s1buf[:], s1buf[:], 1.0 / D, None, ALU.mult), guard=True)
    a_e2 = s_.step("v", [a_m], lambda v: v.tensor_scalar(
        s2buf[:], s2buf[:], 1.0 / D, None, ALU.mult), guard=True)
    a_msq = s_.step("v", [a_m], lambda v: v.scalar_tensor_tensor(
        tmp64a[:], s1buf[:], -1.0, s1buf[:], ALU.mult, ALU.mult), guard=True)
    a_var = s_.step("v", [a_e2, a_msq], lambda v: v.tensor_add(
        s2buf[:], s2buf[:], tmp64a[:]), guard=True)
    a_ln = s_.step("se", [a_var], lambda se: se.activation(
        tmp64a[:], s2buf[:], AF.Ln, bias=cc(0)), guard=True)
    a_ir = s_.step("se", [a_ln], lambda se: se.activation(
        tmp64a[:], tmp64a[:], AF.Exp, scale=cc(2)), guard=True)
    a_mir = s_.step("v", [a_ir], lambda v: v.tensor_mul(
        tmp64b[:], s1buf[:], tmp64a[:]), guard=True)
    a_v1 = s_.step("v", [a_ir], lambda v: v.tensor_mul(
        s2buf[:], s2buf[:], tmp64a[:]), guard=True)
    a_v2 = s_.step("v", [a_v1], lambda v: v.tensor_mul(
        s2buf[:], s2buf[:], tmp64a[:]), guard=True)
    a_irb = s_.step("v", [a_ir], lambda v: v.tensor_copy(
        irbbuf[:], tmp64a[:]), guard=True)
    s_.emit()
    if KPH <= 2:
        return nc

    s_ = sched()
    te_tile = [None] * 64
    for i in range(64):
        xs = xs0 if i % 2 == 0 else xs1
        deps = [te_tile[i - 2]] if i >= 2 else []
        ld = s_.step("g", deps, (lambda xs=xs, i=i: lambda g:
                     g.dma_start(out=xs[:], in_=dXF[128 * i:128 * i + 128, :]))(),
                     sem=f"xs{i % 2}")

        def mk_te(xs=xs, i=i):
            def f(t):
                last = None
                for seg in range(4):
                    last = t.matmul(PS[0:1, 512 * seg:512 * seg + 512],
                                    irbbuf[:, i:i + 1],
                                    xs[:, 512 * seg:512 * seg + 512],
                                    start=(i == 0), stop=(i == 63))
                return last
            return f
        te_tile[i] = s_.step("t", [ld], mk_te(), guard=True)
    b_mirc = s_.step("v", [], lambda v: v.reduce_sum(smc(0), tmp64b[:], AX), guard=True)
    b_virc = s_.step("v", [], lambda v: v.reduce_sum(smc(1), s2buf[:], AX), guard=True)
    b_mircb = s_.step("v", [b_mirc], lambda v: v.tensor_copy(
        invs_colsb[:, 0:1], smc(0)), guard=True)
    b_vircb = s_.step("v", [b_virc], lambda v: v.tensor_copy(
        invs_colsb[:, 1:2], smc(1)), guard=True)
    s_.step("t", [b_mircb, te_tile[63]], lambda t: t.matmul(
        PS[0:1, 2048:2049], invs_colsb[:, 0:1], onescol[:], start=True, stop=True),
        guard=True)
    s_.step("t", [b_vircb], lambda t: t.matmul(
        PS[0:1, 2052:2053], invs_colsb[:, 1:2], onescol[:], start=True, stop=True),
        guard=True)
    s_.emit()
    if KPH <= 3:
        return nc

    s_ = sched()
    c_mir = s_.step("v", [], lambda v: v.tensor_copy(sc(0), PS[0:1, 2048:2049]),
                    guard=True)
    c_vir = s_.step("v", [], lambda v: v.tensor_copy(sc(1), PS[0:1, 2052:2053]),
                    guard=True)
    c_xm = s_.step("v", [c_mir], lambda v: v.tensor_scalar(
        xmrow[:], PS[0:1, 0:2048], sc(0), 1.0 / S, ALU.subtract, ALU.mult),
        guard=True)
    c_xmb = s_.step("v", [c_xm], lambda v: v.tensor_copy(xmrow_b[:], xmrow[:]),
                    guard=True)
    c_sd = s_.step("v", [c_xm], lambda v: v.reduce_sum(sc(2), xmrow[:], AX),
                   guard=True)
    c_mb = s_.step("v", [c_sd], lambda v: v.tensor_scalar(
        sc(3), sc(2), 1.0 / D, None, ALU.mult), guard=True)
    c_mb2 = s_.step("v", [c_mb], lambda v: v.scalar_tensor_tensor(
        sc(4), sc(3), -NTOT / (NTOT - 1.0), sc(3), ALU.mult, ALU.mult), guard=True)
    c_var = s_.step("v", [c_mb2, c_vir], lambda v: v.scalar_tensor_tensor(
        sc(4), sc(1), D / (NTOT - 1.0), sc(4), ALU.mult, ALU.add), guard=True)
    c_vn = s_.step("se", [c_var], lambda se: se.activation(
        sc(5), sc(4), AF.Sigmoid, bias=cc1(3), scale=cc1(4)), guard=True)
    c_xmt = []
    for d in range(16):
        c_xmt.append(s_.step("t", [c_xmb], (lambda d=d: lambda t: t.matmul(
            PS[:, 2056 + d:2057 + d], xmrow_b[0:1, 128 * d:128 * d + 128],
            onescol[0:1, :], start=True, stop=True))(), guard=True))
    c_xmte = s_.step("v", c_xmt, lambda v: v.tensor_copy(
        xmT[:], PS[:, 2056:2072]), guard=True)
    c_hid = []
    for hc in range(4):
        def mk_h(hc=hc):
            def f(t):
                last = None
                for d in range(16):
                    last = t.matmul(
                        PS[:, 2080 + hc:2081 + hc],
                        wc1b[:, 2048 * hc + 128 * d:2048 * hc + 128 * d + 128],
                        xmT[:, d:d + 1], start=(d == 0), stop=(d == 15))
                return last
            return f
        c_hid.append(s_.step("t", [c_xmte], mk_h(), guard=True))
    c_sil = s_.step("se", c_hid, lambda se: se.activation(
        hTb[:], PS[:, 2080:2084], AF.Silu), guard=True)

    def f_logit(t):
        last = None
        for hc in range(4):
            last = t.matmul(PS[0:1, 2088:2089], hTb[:, hc:hc + 1],
                            wc2b[:, hc:hc + 1], start=(hc == 0), stop=(hc == 3))
        return last
    c_log = s_.step("t", [c_sil], f_logit, guard=True)
    c_lrn = s_.step("se", [c_log], lambda se: se.activation(
        sc(6), PS[0:1, 2088:2089], AF.Sigmoid), guard=True)
    c_w0 = s_.step("v", [c_vn, c_lrn], lambda v: v.tensor_add(
        sc(7), sc(5), sc(6)), guard=True)
    c_w = s_.step("v", [c_w0], lambda v: v.tensor_scalar(
        sc(7), sc(7), 96.0, 64.0, ALU.mult, ALU.add), guard=True)
    c_h1 = s_.step("v", [c_w], lambda v: v.tensor_copy(sclb[0:1, 0:1], sc(7)),
                   guard=True)
    c_l1 = s_.step("v", [c_h1], lambda v: v.tensor_sub(
        sc(8), sc(7), sclb[0:1, 0:1]), guard=True)
    c_h2 = s_.step("v", [c_l1], lambda v: v.tensor_copy(sclb[0:1, 1:2], sc(8)),
                   guard=True)
    c_l2 = s_.step("v", [c_h2], lambda v: v.tensor_sub(
        sc(9), sc(8), sclb[0:1, 1:2]), guard=True)
    c_h3 = s_.step("v", [c_l2], lambda v: v.tensor_copy(sclb[0:1, 2:3], sc(9)),
                   guard=True)

    def f_wb(t):
        last = None
        for i in range(3):
            last = t.matmul(PS[:, 2092:2093], onesrow[:],
                            sclb[0:1, i:i + 1], start=(i == 0), stop=(i == 2))
        return last
    c_wb = s_.step("t", [c_h1, c_h2, c_h3], f_wb, guard=True)
    c_w128 = s_.step("v", [c_wb], lambda v: v.tensor_copy(
        smc(2), PS[:, 2092:2093]), guard=True)

    c_m1 = s_.step("v", [c_w128], lambda v: v.tensor_scalar(
        wmf[:], relp[:], smc(2), None, ALU.subtract), guard=True)
    c_m2 = s_.step("v", [c_m1], lambda v: v.tensor_scalar(
        wmf[:], wmf[:], 0.0, BIGNEG, ALU.is_ge, ALU.mult), guard=True)
    c_m3 = s_.step("v", [c_m2], lambda v: v.tensor_add(
        wmf[:], wmf[:], mrel[:]), guard=True)
    c_mg = s_.step("v", [c_m3], lambda v: v.tensor_copy(wm_g[:], wmf[:]),
                   guard=True)
    c_b0 = s_.step("v", [c_m3], lambda v: v.tensor_add(
        bnd0[:], wmf[:], bnd0[:]), guard=True)
    c_m0 = s_.step("v", [c_b0], lambda v: v.tensor_copy(wm_0[:], bnd0[:]),
                   guard=True)
    c_b1 = s_.step("v", [c_m3], lambda v: v.tensor_add(
        bnd1[:], wmf[:], bnd1[:]), guard=True)
    c_m1b = s_.step("v", [c_b1], lambda v: v.tensor_copy(wm_1[:], bnd1[:]),
                    guard=True)
    if "d_w" in dbg:
        s_.step("s", [c_w, c_var, c_mb, c_vn, c_lrn], lambda s: s.dma_start(
            out=dbg["d_w"], in_=scl1[0:1, 0:16]), sem="dbg")
    if "d_colsum" in dbg:
        s_.step("s", [c_xm], lambda s: s.dma_start(out=dbg["d_colsum"],
                                                   in_=xmrow[:]), sem="dbg")
    s_.emit()
    if KPH <= 4:
        return nc

    s_ = sched()
    apply_of = {}
    prev_evac = None
    for jt in range(10):
        xin = xin0 if jt % 2 == 0 else xin1
        src = dX[128 * jt:128 * jt + 128, :] if jt < 8 else \
            dHAL[128 * (jt - 8):128 * (jt - 8) + 128, :]
        dstcol = 256 + 128 * jt if jt < 8 else 128 * (jt - 8)
        deps = [apply_of[jt - 2]] if jt >= 2 else []
        ld = s_.step("s", deps, (lambda xin=xin, src=src: lambda s:
                     s.dma_start(out=xin[:], in_=src))(), sem=f"xi{jt % 2}")
        sm1, sm6 = stats_steps(s_, xin[:], sqscrA[:], [ld])
        sl1 = s_.step("se", [sm6], lambda se: se.activation(
            smc(7), smc(6), AF.Ln, bias=cc(0)), guard=True)
        sl2 = s_.step("se", [sl1], lambda se: se.activation(
            smc(7), smc(7), AF.Exp, scale=cc(2)), guard=True)
        sm7 = s_.step("v", [sl2, sm1], lambda v: v.scalar_tensor_tensor(
            smc(8), smc(4), -1.0, smc(7), ALU.mult, ALU.mult), guard=True)
        ap = s_.step("se", [sm7], (lambda xin=xin: lambda se:
                     se.activation(ln1st[:], xin[:], AF.Identity,
                                   bias=smc(8), scale=smc(7)))(), guard=True)
        apply_of[jt] = ap

        def f_tr(t):
            last = None
            for d in range(16):
                last = t.matmul(PS[:, 128 * d:128 * d + 128],
                                ln1st[:, 128 * d:128 * d + 128],
                                identsb[:], start=True, stop=True)
            return last
        trdeps = [ap] + ([prev_evac] if prev_evac is not None else [])
        tr = s_.step("t", trdeps, f_tr, guard=True)
        ev = s_.step("v", [tr], (lambda dstcol=dstcol: lambda v: v.tensor_copy(
            ln1T[:].rearrange("p (c k) -> p c k", k=KT)[:, :, dstcol:dstcol + 128],
            PS[:, 0:2048].rearrange("p (c k) -> p c k", k=128)))(), guard=True)
        prev_evac = ev
    if "d_ln1t" in dbg:
        s_.step("s", [prev_evac], lambda s: s.dma_start(out=dbg["d_ln1t"],
                                                        in_=ln1T[:]), sem="dbg")
    s_.emit()
    if KPH <= 5:
        return nc

    s_ = sched()

    def qcol(d):
        return 1280 * d + 256

    def kcol(d):
        return 1280 * d
    prev_head_done = None
    te_v_of = {}
    swapmask = []
    for i in range(16):
        swapmask += [2 * i + 1, 2 * i]
    for h in range(HEADS):
        hb = h % 2
        lddeps = [te_v_of[h - 2]] if h >= 2 else []
        lds = []
        for slot, oc in ((0, h), (1, 16 + h), (2, 32 + h)):
            lds.append(s_.step("g", lddeps, (lambda slot=slot, oc=oc, hb=hb:
                lambda g: g.dma_start(
                    out=wsl[hb][slot][:].rearrange("p (d b) -> p d b", b=128),
                    in_=dWQKV[oc].rearrange("d a b -> a d b")))(),
                sem=f"wsl{hb}{slot}"))

        def mk_qkv(which, hb=hb):
            def f(t):
                last = None
                base = [0, 1024, 2560][which]
                for d in range(16):
                    c0 = qcol(d) if which == 0 else kcol(d)
                    w = wsl[hb][which][:, 128 * d:128 * d + 128]
                    segs = ((0, 512), (512, 1024)) if which == 0 else \
                        ((0, 512), (512, 1024), (1024, 1280))
                    for (a, b) in segs:
                        last = t.matmul(PS[:, base + a:base + b], w,
                                        ln1T[:, c0 + a:c0 + b],
                                        start=(d == 0), stop=(d == 15))
                return last
            return f
        depq = [lds[0]] + ([prev_head_done] if prev_head_done is not None else [])
        te_q = s_.step("t", depq, mk_qkv(0), guard=True)
        te_k = s_.step("t", [lds[1]], mk_qkv(1), guard=True)
        te_v = s_.step("t", [lds[2]], mk_qkv(2), guard=True)
        te_v_of[h] = te_v
        ev_q = s_.step("v", [te_q], lambda v: v.tensor_copy(qbb[:], PS[:, 0:1024]),
                       guard=True)
        ev_k = s_.step("v", [te_k], lambda v: v.tensor_copy(kbb[:], PS[:, 1024:2304]),
                       guard=True)
        ev_v = s_.step("v", [te_v], lambda v: v.tensor_copy(vbf[:], PS[:, 2560:3840]),
                       guard=True)

        rq1 = s_.step("v", [ev_q], lambda v: v.stream_shuffle(
            qsh[:], qbb[:], swapmask), guard=True)
        rq2 = s_.step("v", [ev_q], lambda v: v.tensor_mul(
            qrp[:], qbb[:], cosq[:]), guard=True)
        rq2b = s_.step("v", [ev_q], lambda v: v.tensor_mul(
            qtm[:, 0:1024], qbb[:], cosql[:]), guard=True)
        rq2c = s_.step("v", [rq2, rq2b], lambda v: v.tensor_add(
            qrp[:], qrp[:], qtm[:, 0:1024]), guard=True)
        rq3b = s_.step("v", [rq1, rq2c], lambda v: v.tensor_mul(
            qtm[:, 0:1024], qsh[:], sinql[:]), guard=True)
        rq3 = s_.step("v", [rq3b], lambda v: v.tensor_mul(
            qsh[:], qsh[:], sinq[:]), guard=True)
        rq4 = s_.step("v", [rq2c, rq3], lambda v: v.tensor_add(
            qrp[:], qrp[:], qsh[:]), guard=True)
        rope_q = s_.step("v", [rq4, rq3b], lambda v: v.tensor_add(
            qrp[:], qrp[:], qtm[:, 0:1024]), guard=True)

        rk1 = s_.step("v", [ev_k], lambda v: v.stream_shuffle(
            ksh[:], kbb[:], swapmask), guard=True)
        rk2 = s_.step("v", [ev_k], lambda v: v.tensor_mul(
            krp[:], kbb[:], cosk[:]), guard=True)
        rk2b = s_.step("v", [ev_k, rope_q], lambda v: v.tensor_mul(
            qtm[:], kbb[:], coskl[:]), guard=True)
        rk2c = s_.step("v", [rk2, rk2b], lambda v: v.tensor_add(
            krp[:], krp[:], qtm[:]), guard=True)
        rk3b = s_.step("v", [rk1, rk2c], lambda v: v.tensor_mul(
            qtm[:], ksh[:], sinkl[:]), guard=True)
        rk3 = s_.step("v", [rk3b], lambda v: v.tensor_mul(
            ksh[:], ksh[:], sink[:]), guard=True)
        rk4 = s_.step("v", [rk2c, rk3], lambda v: v.tensor_add(
            krp[:], krp[:], ksh[:]), guard=True)
        rope_k = s_.step("v", [rk4, rk3b], lambda v: v.tensor_add(
            krp[:], krp[:], qtm[:]), guard=True)

        def f_vt(t):
            last = None
            for b in range(10):
                last = t.matmul(PS[:, 128 * b:128 * b + 128],
                                vbf[:, 128 * b:128 * b + 128], identsb[:],
                                start=True, stop=True)
            return last
        te_vt = s_.step("t", [ev_q, ev_k, ev_v], f_vt, guard=True)
        ev_vt = s_.step("v", [te_vt], lambda v: v.tensor_copy(
            vtb[:], PS[:, 0:1280]), guard=True)

        def f_sc(t):
            last = None
            for j in range(8):
                t.matmul(PS[:, 512 * j:512 * j + 384],
                         qrp[:, 128 * j:128 * j + 128],
                         krp[:, 128 * j:128 * j + 384],
                         start=True, stop=False)
                wm = wm_0 if j == 0 else (wm_1 if j == 1 else wm_g)
                last = t.matmul(PS[:, 512 * j:512 * j + 384], identsb[:],
                                wm[:], start=False, stop=True)
            return last
        te_sc = s_.step("t", [rope_q, rope_k, ev_vt, ev_v], f_sc, guard=True)
        ve_rm = s_.step("v", [te_sc], lambda v: v.reduce_max(
            rmaxn[:].rearrange("p (s o) -> p s o", o=1),
            PS[:].rearrange("p (s k) -> p s k", k=512)[:, :, 0:384],
            AX, negate=True), guard=True)
        def f_exp(se):
            last = None
            for j in range(8):
                last = se.activation(pe[:, BAND * j:BAND * j + BAND],
                                     PS[:, 512 * j:512 * j + 384], AF.Exp,
                                     bias=rmaxn[:, j:j + 1], scale=1.0,
                                     accum_out=rsum[:, j:j + 1])
            return last
        se_exp = s_.step("se", [ve_rm], f_exp, guard=True)
        ve_ri = s_.step("v", [se_exp], lambda v: v.reciprocal(rinv[:], rsum[:]),
                        guard=True)

        def f_pn(v):
            last = None
            for j in range(8):
                last = v.tensor_scalar(pe[:, BAND * j:BAND * j + BAND],
                                       pe[:, BAND * j:BAND * j + BAND],
                                       rinv[:, j:j + 1], None, ALU.mult)
            return last
        ve_pn = s_.step("v", [ve_ri], f_pn, guard=True)

        def f_pt(t):
            last = None
            for j in range(8):
                for kk in range(3):
                    m = 3 * j + kk
                    last = t.matmul(PS[:, 128 * m:128 * m + 128],
                                    pe[:, BAND * j + 128 * kk:BAND * j + 128 * kk + 128],
                                    identsb[:], start=True, stop=True)
            return last
        te_pt = s_.step("t", [ve_pn, se_exp], f_pt, guard=True)
        ve_pte = s_.step("v", [te_pt], lambda v: v.tensor_copy(
            ptb[:], PS[:, 0:3072]), guard=True)

        def f_ao(t):
            last = None
            for j in range(8):
                for kk in range(3):
                    m = 3 * j + kk
                    last = t.matmul(PS[:, 3072 + 128 * j:3072 + 128 * j + 128],
                                    vtb[:, 128 * (j + kk):128 * (j + kk) + 128],
                                    ptb[:, 128 * m:128 * m + 128],
                                    start=(kk == 0), stop=(kk == 2))
            return last
        te_ao = s_.step("t", [ve_pte], f_ao, guard=True)
        ve_aoe = s_.step("v", [te_ao], (lambda h=h: lambda v: v.tensor_copy(
            attnout[:, 1024 * h:1024 * h + 1024], PS[:, 3072:4096]))(), guard=True)
        prev_head_done = ve_aoe
    if "d_attnout" in dbg:
        s_.step("s", [prev_head_done], lambda s: s.dma_start(
            out=dbg["d_attnout"], in_=attnout[:]), sem="dbg")
    s_.emit()
    if KPH <= 6:
        return nc

    s_ = sched()
    ev_of = {}
    prev_ev = None
    for o in range(16):
        lddeps = [ev_of[o - 2]] if o >= 2 else []
        ld = s_.step("g", lddeps, (lambda o=o: lambda g: g.dma_start(
            out=wsl[1][o % 2][:].rearrange("p (h b) -> p h b", b=128),
            in_=dWOUT[o].rearrange("h a b -> a h b")))(), sem=f"wo{o % 2}")

        def f_opj(t, o=o):
            last = None
            for hh in range(16):
                w = wsl[1][o % 2][:, 128 * hh:128 * hh + 128]
                for (a, b) in ((0, 512), (512, 1024)):
                    last = t.matmul(PS[:, a:b], w,
                                    attnout[:, 1024 * hh + a:1024 * hh + b],
                                    start=(hh == 0), stop=(hh == 15))
            return last
        dep = [ld] + ([prev_ev] if prev_ev is not None else [])
        te = s_.step("t", dep, f_opj, guard=True)
        ev = s_.step("v", [te], (lambda o=o: lambda v: v.tensor_copy(
            opj[:, 1024 * o:1024 * o + 1024], PS[:, 0:1024]))(), guard=True)
        prev_ev = ev
        ev_of[o] = ev
    opj_done = prev_ev
    resid_of = {}
    prev_apply = None
    for jt in (range(8) if KPH != 61 else []):
        xin = xin0 if jt % 2 == 0 else xin1
        lddeps = [resid_of[jt - 2]] if jt >= 2 else []
        ld = s_.step("s", lddeps, (lambda xin=xin, jt=jt: lambda s: s.dma_start(
            out=xin[:], in_=dX[128 * jt:128 * jt + 128, :]))(), sem=f"xi{jt % 2}")

        def f_tr(t, jt=jt):
            last = None
            for o in range(16):
                last = t.matmul(PS[:, 2048 + 128 * o:2048 + 128 * o + 128],
                                opj[:, 1024 * o + 128 * jt:1024 * o + 128 * jt + 128],
                                identsb[:], start=True, stop=True)
            return last
        trdeps = [opj_done] + ([prev_apply] if prev_apply is not None else [])
        tr = s_.step("t", trdeps, f_tr, guard=True)
        if KPH == 62:
            prev_apply = tr
            resid_of[jt] = tr
            continue
        if KPH == 631:
            u6 = s_.step("v", [tr], lambda v: v.reduce_sum(
                smc(20), PS[:, 2048:4096], AX), guard=True)
            prev_apply = u6
            resid_of[jt] = u6
            continue
        if KPH == 632:
            u6 = s_.step("se", [tr], lambda se: se.activation(
                sqscrB[:], PS[:, 2048:4096], AF.Square, accum_out=smc(21)),
                guard=True)
            prev_apply = u6
            resid_of[jt] = u6
            continue
        u1, u6 = stats_steps(s_, PS[:, 2048:4096], sqscrB[:], [tr])
        if KPH == 63:
            prev_apply = u6
            resid_of[jt] = u6
            continue
        w1 = s_.step("se", [u6], lambda se: se.activation(
            smc(7), smc(6), AF.Ln, bias=cc(0)), guard=True)
        w2 = s_.step("se", [w1], lambda se: se.activation(
            smc(7), smc(7), AF.Exp, scale=cc(2)), guard=True)
        u7 = s_.step("v", [w2, u1], lambda v: v.scalar_tensor_tensor(
            smc(8), smc(4), -1.0, smc(7), ALU.mult, ALU.mult), guard=True)
        ap = s_.step("se", [u7], lambda se: se.activation(
            bstage[:], PS[:, 2048:4096], AF.Identity, bias=smc(8), scale=smc(7)),
            guard=True)
        prev_apply = ap
        if KPH == 64:
            resid_of[jt] = ap
            continue
        rs = s_.step("v", [ap, ld], (lambda xin=xin, jt=jt: lambda v:
            v.tensor_add(x2res[:, 2048 * jt:2048 * jt + 2048], xin[:], bstage[:]))(),
            guard=True)
        resid_of[jt] = rs
    if "d_x2" in dbg:
        for jt in range(8):
            s_.step("s", [resid_of[7]], (lambda jt=jt: lambda s: s.dma_start(
                out=dbg["d_x2"][128 * jt:128 * jt + 128, :],
                in_=x2res[:, 2048 * jt:2048 * jt + 2048]))(), sem="dbg")
    s_.emit()
    if KPH <= 7 or KPH in (61, 62, 63, 64, 631, 632):
        return nc

    sd_half0_mark = None
    for H in range(2):
        s_ = sched()
        prev_trk_ev = None
        xqt_done = []
        kdone_all = []
        for lt in range(NT_H):
            tj = NT_H * H + lt
            x2t = x2res[:, 2048 * tj:2048 * tj + 2048]
            sdeps = [xqt_done[-1]] if xqt_done else []
            q1, q6 = stats_steps(s_, x2t, sqscrC1[:], sdeps)
            bn = q1
            q7 = s_.step("se", [q6], lambda se: se.activation(
                smc(7), smc(6), AF.Ln, bias=cc(0)), guard=True)
            q8 = s_.step("se", [q7], lambda se: se.activation(
                smc(7), smc(7), AF.Exp, scale=cc(2)), guard=True)
            q9 = s_.step("v", [q8], lambda v: v.tensor_mul(
                smc(9), smc(6), smc(7)), guard=True)
            q10 = s_.step("v", [q9], lambda v: v.tensor_mul(
                smc(9), smc(9), smc(7)), guard=True)
            q11 = s_.step("se", [q10], lambda se: se.activation(
                smc(10), smc(9), AF.Ln, bias=cc(1)), guard=True)
            q12 = s_.step("se", [q11], lambda se: se.activation(
                smc(10), smc(10), AF.Exp, scale=cc(2)), guard=True)
            q13 = s_.step("v", [q12, q8], lambda v: v.tensor_mul(
                smc(11), smc(7), smc(10)), guard=True)
            qmx = s_.step("v", [bn], (lambda x2t=x2t: lambda v:
                          v.reduce_max(smc(12), x2t, AX))(), guard=True)
            qmn = s_.step("v", [qmx], (lambda x2t=x2t: lambda v:
                          v.tensor_reduce(smc(13), x2t, AX, ALU.min))(), guard=True)
            qa1 = s_.step("v", [qmx, q1], lambda v: v.tensor_sub(
                smc(14), smc(12), smc(4)), guard=True)
            qa2 = s_.step("v", [qmn, q1], lambda v: v.tensor_sub(
                smc(15), smc(4), smc(13)), guard=True)
            qam = s_.step("v", [qa1, qa2], lambda v: v.tensor_max(
                smc(14), smc(14), smc(15)), guard=True)
            qab = s_.step("v", [qam, q13], lambda v: v.tensor_mul(
                smc(16), smc(14), smc(11)), guard=True)
            qrc = s_.step("v", [qab], lambda v: v.reciprocal(
                smc(17), smc(16)), guard=True)
            qsx = s_.step("v", [qrc], lambda v: v.tensor_scalar(
                smc(17), smc(17), 127.0, None, ALU.mult), guard=True)
            qsy = s_.step("v", [qsx, q13], lambda v: v.tensor_mul(
                smc(18), smc(17), smc(11)), guard=True)
            qbi = s_.step("v", [qsy, q1], lambda v: v.scalar_tensor_tensor(
                smc(19), smc(4), -1.0, smc(18), ALU.mult, ALU.mult), guard=True)
            qb2 = s_.step("v", [qbi], lambda v: v.tensor_scalar(
                smc(19), smc(19), 128.0, None, ALU.add), guard=True)
            qiv = s_.step("v", [qab], (lambda tj=tj: lambda v: v.tensor_scalar(
                invs_cols[:, tj:tj + 1], smc(16), 1.0 / 127.0, None, ALU.mult))(),
                guard=True)
            kdone = []
            for hh in range(2):
                xsl = x2res[:, 2048 * tj + 1024 * hh:2048 * tj + 1024 * hh + 1024]
                up = s_.step("se", [qb2], (lambda xsl=xsl, hh=hh: lambda se:
                    se.activation(qu[:, 1024 * hh:1024 * hh + 1024], xsl, AF.Identity,
                                  bias=smc(19), scale=smc(18)))(), guard=True)

                uap = qu[:, 1024 * hh:1024 * hh + 1024]
                cdeps = [up] + (kdone_all[-1:] if kdone_all else [])
                cprev = s_.step("v", cdeps, (lambda uap=uap: lambda v:
                    v.tensor_copy(qrr[:], uap))(), guard=True)
                for b in (128, 64, 32, 16, 8, 4, 2, 1):
                    c_ts = s_.step("v", [cprev], (lambda b=b: lambda v:
                        v.tensor_scalar(qtb[:], qrr[:], float(b) - 0.5, float(b),
                                        ALU.is_ge, ALU.mult))(), guard=True)
                    cprev = s_.step("v", [c_ts], lambda v:
                        v.tensor_sub(qrr[:], qrr[:], qtb[:]), guard=True)
                cprev = s_.step("v", [cprev], (lambda uap=uap: lambda v:
                    v.tensor_sub(qrr[:], uap, qrr[:]))(), guard=True)
                kd = s_.step("v", [cprev], (lambda hh=hh: lambda v:
                    v.tensor_scalar(kbf[:, 1024 * hh:1024 * hh + 1024],
                                    qrr[:], 128.0, None, ALU.subtract))(),
                    guard=True)
                kdone.append(kd)
                kdone_all.append(kd)

            def f_trk(t):
                last = None
                for d in range(16):
                    last = t.matmul(PS[:, 128 * d:128 * d + 128],
                                    kbf[:, 128 * d:128 * d + 128],
                                    identsb[:], start=True, stop=True)
                return last
            trdeps = list(kdone) + ([prev_trk_ev] if prev_trk_ev is not None else [])
            trk = s_.step("t", trdeps, f_trk, guard=True)
            evk = s_.step("v", [trk], (lambda lt=lt: lambda v: v.tensor_copy(
                xqT[:].rearrange("p (c k) -> p c k", k=THALF)[:, :, 128 * lt:128 * lt + 128],
                PS[:, 0:2048].rearrange("p (c k) -> p c k", k=128)))(), guard=True)
            prev_trk_ev = evk
            xqt_done.append(evk)
        quant_end = xqt_done[-1]
        iv0 = s_.step("v", [quant_end], (lambda H=H: lambda v: v.tensor_copy(
            invs_colsb[:, NT_H * H:NT_H * H + NT_H],
            invs_cols[:, NT_H * H:NT_H * H + NT_H]))(), guard=True)
        ivt = []
        for lt in range(NT_H):
            tj = NT_H * H + lt
            ivt.append(s_.step("t", [iv0], (lambda tj=tj, lt=lt: lambda t: t.matmul(
                PS[0:1, 3072 + 128 * lt:3072 + 128 * lt + 128],
                invs_colsb[:, tj:tj + 1], identsb[:], start=True, stop=True))(),
                guard=True))
        iv1 = s_.step("v", ivt, lambda v: v.tensor_copy(
            srowb[0:1, :], PS[0:1, 3072:3584]), guard=True)
        ivb = s_.step("t", [iv1], lambda t: t.matmul(
            PS[:, 3584:4096], onesrow[:], srowb[0:1, :], start=True, stop=True),
            guard=True)
        iv2 = s_.step("v", [ivb], lambda v: v.tensor_copy(
            invsb[:], PS[:, 3584:4096]), guard=True)
        local_dbg = []
        if H == 0 and "d_xqt" in dbg:
            local_dbg.append(s_.step("s", [quant_end], lambda s: s.dma_start(
                out=dbg["d_xqt"], in_=xqT[:]), sem="dbg"))
        if H == 0 and "d_invs" in dbg:
            local_dbg.append(s_.step("s", [iv2], lambda s: s.dma_start(
                out=dbg["d_invs"], in_=invs_cols[:]), sem="dbg"))

        te_gu_of = {}
        pp_of = {}
        gust_of = {}
        prev_sgn = None
        for hc in range(NHC):
            hb = hc % 2
            lddeps = [te_gu_of[hc - 2]] if hc >= 2 else [quant_end]
            if H == 0 and hc >= 2:
                lddeps = lddeps + gust_of[hc - 2]
            if H == 0:
                ldg_ = s_.step("g", lddeps, (lambda hc=hc, hb=hb: lambda g:
                    g.dma_start(out=wgl[hb][:].rearrange("p (d b) -> p d b", b=128),
                                in_=dWG[hc].rearrange("d a b -> a d b")))(),
                    sem=f"wg{hb}")
                ldu_ = s_.step("s", [prev_sgn] if prev_sgn is not None
                               else [quant_end],
                    (lambda hc=hc: lambda s:
                    s.dma_start(out=ustg[:].rearrange("p (d b) -> p d b", b=128),
                                in_=dWU[hc].rearrange("d a b -> a d b")))(),
                    sem="wuf")
                sgn = s_.step("se", [ldg_, ldu_], (lambda hb=hb: lambda se:
                    (se.activation(wgl[hb][:], wgl[hb][:], AF.Sign),
                     se.activation(wul[hb][:], ustg[:], AF.Sign))[-1])(),
                    guard=True)
                prev_sgn = sgn
                stg1 = s_.step("s", [sgn], (lambda hc=hc, hb=hb: lambda s:
                    s.dma_start(out=dSGG[hc].rearrange("(a c) -> a c", a=128),
                                in_=wgl[hb][:]))(), sem=f"sgstg{hb}")
                stg2 = s_.step("s", [sgn], (lambda hc=hc, hb=hb: lambda s:
                    s.dma_start(out=dSGU[hc].rearrange("(a c) -> a c", a=128),
                                in_=wul[hb][:]))(), sem=f"sgstu{hb}")
                gust_of[hc] = [stg1, stg2]
                wdep = [sgn]
            else:
                mark = sd_half0_mark

                def mk_ldg(hc=hc, hb=hb, mark=mark):
                    def f(s):
                        if hc == 0 and mark is not None:
                            for kk, vv in mark.items():
                                s.wait_ge(sems[kk], vv)
                        return s.dma_start(
                            out=wgl[hb][:],
                            in_=dSGG[hc].rearrange("(a c) -> a c", a=128))
                    return f
                ldg_ = s_.step("s", lddeps, mk_ldg(), sem=f"wgh{hb}")
                ldu_ = s_.step("s", lddeps, (lambda hc=hc, hb=hb: lambda s:
                    s.dma_start(out=wul[hb][:],
                                in_=dSGU[hc].rearrange("(a c) -> a c", a=128)))(),
                    sem=f"wuh{hb}")
                wdep = [ldg_, ldu_]

            def f_gu(t, hb=hb):
                last = None
                for d in range(16):
                    last = t.matmul(PS[:, 0:512], wgl[hb][:, 128 * d:128 * d + 128],
                                    xqT[:, 512 * d:512 * d + 512],
                                    start=(d == 0), stop=(d == 15))
                for d in range(16):
                    last = t.matmul(PS[:, 512:1024], wul[hb][:, 128 * d:128 * d + 128],
                                    xqT[:, 512 * d:512 * d + 512],
                                    start=(d == 0), stop=(d == 15))
                return last
            gudeps = list(wdep) + [iv2]
            if hc >= 1:
                gudeps.append(pp_of[hc - 1])
            te_gu = s_.step("t", gudeps, f_gu, guard=True)
            te_gu_of[hc] = te_gu
            g1 = s_.step("v", [te_gu], lambda v: v.tensor_mul(
                gsc[:], PS[:, 0:512], invsb[:]), guard=True)
            g2 = s_.step("se", [g1], lambda se: se.activation(
                silb[:], gsc[:], AF.Silu), guard=True)
            pp = s_.step("v", [g2], (lambda hc=hc: lambda v: v.tensor_mul(
                pprim[:, 512 * hc:512 * hc + 512], silb[:], PS[:, 512:1024]))(),
                guard=True)
            pp_of[hc] = pp
        pp_all = pp_of[NHC - 1]

        ev_wd = {}
        last_q_of = {}
        wdst_of = {}
        for o in range(16):
            for qq in range(4):
                key = o * 4 + qq
                hb = key % 2
                lddeps = [last_q_of[key - 2]] if key >= 2 else [quant_end]
                if H == 0 and key >= 2:
                    lddeps = lddeps + [wdst_of[key - 2]]
                if H == 0 and key < 2 and prev_sgn is not None:
                    lddeps = lddeps + [prev_sgn]
                if H == 0:
                    ldd = s_.step("g", lddeps, (lambda o=o, qq=qq, hb=hb: lambda g:
                        g.dma_start(out=wdl[hb][:].rearrange("p (d b) -> p d b", b=128),
                                    in_=dWD[o, 16 * qq:16 * qq + 16]
                                    .rearrange("d a b -> a d b")))(), sem=f"wd{hb}")
                    sgd = s_.step("se", [ldd], (lambda hb=hb: lambda se:
                        se.activation(wdl[hb][:], wdl[hb][:], AF.Sign))(), guard=True)
                    wdst_of[key] = s_.step("s", [sgd], (lambda o=o, qq=qq, hb=hb:
                        lambda s: s.dma_start(
                            out=dSGD[o, 262144 * qq:262144 * (qq + 1)]
                            .rearrange("(a c) -> a c", a=128),
                            in_=wdl[hb][:]))(), sem=f"sgstd{hb}")
                    wdep2 = [sgd]
                else:
                    ldd = s_.step("s", lddeps, (lambda o=o, qq=qq, hb=hb: lambda s:
                        s.dma_start(out=wdl[hb][:],
                                    in_=dSGD[o, 262144 * qq:262144 * (qq + 1)]
                                    .rearrange("(a c) -> a c", a=128)))(), sem=f"wdh{hb}")
                    wdep2 = [ldd]

                def f_wd(t, o=o, qq=qq, hb=hb):
                    last = None
                    bank = o % 8
                    for dd in range(16):
                        hcg = 16 * qq + dd
                        last = t.matmul(
                            PS[:, 512 * bank:512 * bank + 512],
                            wdl[hb][:, 128 * dd:128 * dd + 128],
                            pprim[:, 512 * hcg:512 * hcg + 512],
                            start=(qq == 0 and dd == 0),
                            stop=(qq == 3 and dd == 15))
                    return last
                wdd = wdep2 + [pp_all]
                if o >= 8 and qq == 0:
                    wdd.append(ev_wd[o - 8])
                tq = s_.step("t", wdd, f_wd, guard=True)
                last_q_of[key] = tq
            ev_wd[o] = s_.step("v", [last_q_of[o * 4 + 3]], (lambda o=o: lambda v:
                v.tensor_copy(ffnout[:, 512 * o:512 * o + 512],
                              PS[:, 512 * (o % 8):512 * (o % 8) + 512]))(),
                guard=True)

        prev_ap2 = None
        store_of = {}
        for lt in range(NT_H):
            tj = NT_H * H + lt
            cstg = cst0 if lt % 2 == 0 else cst1

            def f_trf(t, lt=lt):
                last = None
                for o in range(16):
                    last = t.matmul(
                        PS[:, 128 * o:128 * o + 128],
                        ffnout[:, 512 * o + 128 * lt:512 * o + 128 * lt + 128],
                        identsb[:], start=True, stop=True)
                return last
            trdeps = [ev_wd[o] for o in range(16)]
            if prev_ap2 is not None:
                trdeps.append(prev_ap2)
            if H == 0 and lt == 0:
                trdeps += [wdst_of[62], wdst_of[63]] + local_dbg[-1:]
            trf = s_.step("t", trdeps, f_trf, guard=True)
            e1, e6 = stats_steps(s_, PS[:, 0:2048], sqscrC5[:], [trf])
            e7 = s_.step("v", [e1], (lambda tj=tj: lambda v: v.tensor_mul(
                smc(4), smc(4), invs_cols[:, tj:tj + 1]))(), guard=True)
            e8 = s_.step("v", [e6], (lambda tj=tj: lambda v: v.tensor_mul(
                smc(6), smc(6), invs_cols[:, tj:tj + 1]))(), guard=True)
            e9 = s_.step("v", [e8], (lambda tj=tj: lambda v: v.tensor_mul(
                smc(6), smc(6), invs_cols[:, tj:tj + 1]))(), guard=True)
            f1 = s_.step("se", [e9], lambda se: se.activation(
                smc(7), smc(6), AF.Ln, bias=cc(0)), guard=True)
            f2 = s_.step("se", [f1], lambda se: se.activation(
                smc(7), smc(7), AF.Exp, scale=cc(2)), guard=True)
            e10 = s_.step("v", [f2], (lambda tj=tj: lambda v: v.tensor_mul(
                smc(9), smc(7), invs_cols[:, tj:tj + 1]))(), guard=True)
            e11 = s_.step("v", [e7, f2], lambda v: v.scalar_tensor_tensor(
                smc(8), smc(4), -1.0, smc(7), ALU.mult, ALU.mult), guard=True)
            apdeps = [e10, e11]
            if lt >= 2:
                apdeps.append(store_of[lt - 2])
            ap2 = s_.step("se", apdeps, (lambda cstg=cstg: lambda se:
                se.activation(cstg[:], PS[:, 0:2048], AF.Identity,
                              bias=smc(8), scale=smc(9)))(), guard=True)
            prev_ap2 = ap2
            rs2 = s_.step("v", [ap2], (lambda cstg=cstg, tj=tj: lambda v:
                v.tensor_add(cstg[:], x2res[:, 2048 * tj:2048 * tj + 2048],
                             cstg[:]))(), guard=True)
            st = s_.step("s", [rs2], (lambda cstg=cstg, tj=tj: lambda s:
                s.dma_start(out=dOUT[128 * tj:128 * tj + 128, :], in_=cstg[:]))(),
                sem=f"ost{lt % 2}")
            store_of[lt] = st

        def f_final(s):
            s.wait_ge(sems["ost0"], cnt_snapshot["ost0"])
            s.wait_ge(sems["ost1"], cnt_snapshot["ost1"])
            if "dbg" in sems:
                s.wait_ge(sems["dbg"], cnt_snapshot.get("dbg", 0))
            return None
        cnt_snapshot = {"ost0": cnt.get("ost0", 0) + 16 * NT_H // 2 * 2 // 2,
                        "ost1": cnt.get("ost1", 0) + 16 * NT_H // 2,
                        "dbg": cnt.get("dbg", 0)}
        s_.step("s", [store_of[NT_H - 1], store_of[NT_H - 2]], f_final, noinc=True)
        s_.emit()
        if H == 0 and KPH <= 8:
            return nc
        if H == 0:
            sd_half0_mark = {k: cnt.get(k, 0)
                             for k in ("sgstg0", "sgstg1", "sgstu0", "sgstu1",
                                       "sgstd0", "sgstd1")}

    return nc


def _blocked(w, rows_per=128, cols_per=128):
    R, C = w.shape
    return np.ascontiguousarray(
        w.reshape(R // rows_per, rows_per, C // cols_per, cols_per)
        .transpose(2, 0, 1, 3))


def kernel(L0_attn_x, L0_attn_w_qkv, L0_attn_w_out, L0_attn_w_c1, L0_attn_w_c2,
           L0_ffn_W_g, L0_ffn_W_u, L0_ffn_W_d):
    x = np.ascontiguousarray(np.asarray(L0_attn_x, dtype=np.float32)[0])
    wqkv_s = _blocked(np.ascontiguousarray(np.asarray(L0_attn_w_qkv, np.float32).T))
    wout_s = _blocked(np.ascontiguousarray(np.asarray(L0_attn_w_out, np.float32).T))
    wc1_s = np.ascontiguousarray(
        np.asarray(L0_attn_w_c1, np.float32).T
        .reshape(16, 128, 4, 128).transpose(2, 0, 1, 3))
    wc2_s = np.ascontiguousarray(
        np.asarray(L0_attn_w_c2, np.float32).T.reshape(4, 128, 1))
    wg_s = _blocked(np.ascontiguousarray(np.asarray(L0_ffn_W_g, np.float32).T))
    wu_s = _blocked(np.ascontiguousarray(np.asarray(L0_ffn_W_u, np.float32).T))
    wd_s = _blocked(np.ascontiguousarray(np.asarray(L0_ffn_W_d, np.float32).T))

    import ctypes
    _e = np.arange(0, HD, 2, dtype=np.float32).astype(np.float64) / HD
    _pw = (np.float64(10000.0) ** _e).astype(np.float32)
    inv_freq32 = (1.0 / _pw.astype(np.float64)).astype(np.float32)
    _posg = np.arange(-HALO, S, dtype=np.float32).astype(np.float64)
    freqs32 = (inv_freq32.astype(np.float64)[:, None] * _posg[None, :]
               ).astype(np.float32)
    _libm = ctypes.CDLL("libm.so.6")
    _libm.cosf.restype = ctypes.c_float
    _libm.cosf.argtypes = [ctypes.c_float]
    _libm.sinf.restype = ctypes.c_float
    _libm.sinf.argtypes = [ctypes.c_float]
    _ff = freqs32.ravel()
    cos32g = np.fromiter((_libm.cosf(float(v)) for v in _ff), np.float32,
                         _ff.size).reshape(freqs32.shape)
    sin32g = np.fromiter((_libm.sinf(float(v)) for v in _ff), np.float32,
                         _ff.size).reshape(freqs32.shape)
    cos32g = np.concatenate([cos32g, cos32g], axis=0)
    sin32g = np.concatenate([sin32g, sin32g], axis=0)
    sgn = np.where(np.arange(HD) % 2 == 0, -1.0, 1.0)

    relp1 = (257.0 + np.arange(128)[:, None] - np.arange(BAND)[None, :]).astype(np.float32)
    mrel0 = np.where(relp1 - 1.0 >= 1.0, 0.0, BIGNEG).astype(np.float32)
    mrel0 = np.where(relp1 >= 1.0, 0.0, BIGNEG).astype(np.float32)

    in_maps = []
    for c in range(NCORES):
        start = TOK * c
        cos64 = cos32g[:, start:start + HALO + TOK].astype(np.float64)
        sin64 = sin32g[:, start:start + HALO + TOK].astype(np.float64) * sgn[:, None]
        cosk_ = cos64.astype(np.float32)
        sink_ = sin64.astype(np.float32)
        coskl_ = (cos64 - cosk_).astype(np.float32)
        sinkl_ = (sin64 - sink_).astype(np.float32)
        sc64 = float(HD) ** -0.5
        cq64 = cos64[:, HALO:] * sc64
        sq64 = sin64[:, HALO:] * sc64
        cosq_ = cq64.astype(np.float32)
        sinq_ = sq64.astype(np.float32)
        cosql_ = (cq64 - cosq_).astype(np.float32)
        sinql_ = (sq64 - sinq_).astype(np.float32)
        if c == 0:
            halo = np.zeros((HALO, D), np.float32)
            b0 = np.broadcast_to(
                np.where(np.arange(BAND)[None, :] < 256, BIGNEG, 0.0),
                (128, BAND)).astype(np.float32)
            b1 = np.broadcast_to(
                np.where(np.arange(BAND)[None, :] < 128, BIGNEG, 0.0),
                (128, BAND)).astype(np.float32)
        else:
            halo = np.ascontiguousarray(x[start - HALO:start])
            b0 = np.zeros((128, BAND), np.float32)
            b1 = np.zeros((128, BAND), np.float32)
        in_maps.append({
            "x_own": np.ascontiguousarray(x[start:start + TOK]),
            "x_halo": halo, "x_full": x,
            "wqkv_s": wqkv_s, "wout_s": wout_s, "wc1_s": wc1_s, "wc2_s": wc2_s,
            "wg_s": wg_s, "wu_s": wu_s, "wd_s": wd_s,
            "cosk": cosk_, "sink": sink_, "coskl": coskl_, "sinkl": sinkl_,
            "cosq": cosq_, "sinq": sinq_, "cosql": cosql_, "sinql": sinql_,
            "relp1": relp1, "mrel0": mrel0, "bnd0": b0, "bnd1": b1,
        })

    nc = build_program()
    res = run_bass_kernel_spmd(nc, in_maps, core_ids=list(range(NCORES)))
    out = np.concatenate([np.asarray(res.results[c]["out"]) for c in range(NCORES)],
                         axis=0)
    if DBG:
        kernel.dbg_results = res.results
    return out.reshape(1, S, D).astype(np.float32)


# version 04 (final)
# speedup vs baseline: 1.1810x; 1.1810x over previous
import numpy as np
import ml_dtypes
import concourse.bass as bass
import concourse.mybir as mybir
from concourse.bass_utils import run_bass_kernel_spmd

F32 = mybir.dt.float32
BF16 = mybir.dt.bfloat16
AF = mybir.ActivationFunctionType
ALU = mybir.AluOpType
AX = mybir.AxisListType.X

NCORES = 8
S = 8192
D = 2048
TOK = 1024
HALO = 256
KT = TOK + HALO
NT = 8
HEADS = 16
HD = 128
HID = 8192
NHC = 64
NDC = 16
BAND = 384
BIGNEG = -float(2.0 ** 100)
SCALE_ATTN = float(HD) ** -0.5
EPS1 = 1e-5
EPS2 = 1e-8
THALF = 512
NT_H = 4
NTOT = float(S * D)

DBG = []


class Sched:
    ENGS = ("t", "v", "se", "g", "s")

    def __init__(self, nc, sems, cnt):
        self.nc = nc
        self.sems = sems
        self.cnt = cnt
        self.steps = []

    def step(self, eng, deps, fn, guard=False, noinc=False, sem=None):
        assert eng in self.ENGS
        if eng in ("g", "s") and not noinc:
            assert sem is not None, "DMA steps need a slot sem"
        else:
            sem = eng
        if sem not in self.sems:
            self.sems[sem] = self.nc.alloc_semaphore("sem_" + sem)
            self.cnt[sem] = 0
        for d in deps:
            assert 0 <= d < len(self.steps), (d, len(self.steps))
        self.steps.append([eng, list(deps), fn, guard, noinc, sem])
        return len(self.steps) - 1

    def emit(self):
        n = len(self.steps)
        synced = [False] * n
        for eng, deps, fn, guard, noinc, sem in self.steps:
            for d in deps:
                if self.steps[d][0] != eng or self.steps[d][3] or guard:
                    synced[d] = True
        for i, st in enumerate(self.steps):
            if st[0] in ("g", "s") and not st[4]:
                synced[i] = True
            if st[4]:
                assert not synced[i], ("noinc step has dependents", i)
        post = [0] * n
        c = dict(self.cnt)
        for i, st in enumerate(self.steps):
            if synced[i]:
                c[st[5]] += 16 if st[0] in ("g", "s") else 1
            post[i] = c[st[5]]
        per_eng = {e: [] for e in self.ENGS}
        for i, st in enumerate(self.steps):
            per_eng[st[0]].append(i)
        sems, steps = self.sems, self.steps

        dbg_mark = self.cnt.get("dbg")

        def mk(ek):
            ids = per_eng[ek]

            def body(eng):
                if dbg_mark and ek in ("t", "v", "se"):
                    eng.wait_ge(sems["dbg"], dbg_mark)
                for i in ids:
                    _, deps, fn, guard, _noinc, _sem = steps[i]
                    for d in deps:
                        dk = steps[d][0]
                        if dk != ek or steps[d][3] or guard:
                            assert synced[d], (i, d)
                            eng.wait_ge(sems[steps[d][5]], post[d])
                    last = fn(eng)
                    if synced[i]:
                        assert last is not None, ("step needs inc", i)
                        last.then_inc(sems[steps[i][5]],
                                      16 if ek in ("g", "s") else 1)
            return body

        with self.nc.Block() as blk:
            if per_eng["t"]:
                blk.tensor(mk("t"))
            if per_eng["v"]:
                blk.vector(mk("v"))
            if per_eng["se"]:
                blk.scalar(mk("se"))
            if per_eng["g"]:
                blk.gpsimd(mk("g"))
            if per_eng["s"]:
                blk.sync(mk("s"))
        self.cnt.update(c)


def build_program():
    import os
    KPH = int(os.environ.get("KPH", "9"))
    nc = bass.Bass("TRN2", target_bir_lowering=False)

    dX = nc.dram_tensor("x_own", [TOK, D], F32, kind="ExternalInput").ap()
    dHAL = nc.dram_tensor("x_halo", [HALO, D], F32, kind="ExternalInput").ap()
    dXF = nc.dram_tensor("x_full", [S, D], F32, kind="ExternalInput").ap()
    dWQKV = nc.dram_tensor("wqkv_s", [48, NDC, 128, 128], F32, kind="ExternalInput").ap()
    dWOUT = nc.dram_tensor("wout_s", [16, 16, 128, 128], F32, kind="ExternalInput").ap()
    dWC1 = nc.dram_tensor("wc1_s", [4, NDC, 128, 128], F32, kind="ExternalInput").ap()
    dWC2 = nc.dram_tensor("wc2_s", [4, 128, 1], F32, kind="ExternalInput").ap()
    dWG = nc.dram_tensor("wg_s", [NHC, NDC, 128, 128], F32, kind="ExternalInput").ap()
    dWU = nc.dram_tensor("wu_s", [NHC, NDC, 128, 128], F32, kind="ExternalInput").ap()
    dWD = nc.dram_tensor("wd_s", [16, NHC, 128, 128], F32, kind="ExternalInput").ap()
    dCOS = nc.dram_tensor("cosk", [HD, KT], F32, kind="ExternalInput").ap()
    dSIN = nc.dram_tensor("sink", [HD, KT], F32, kind="ExternalInput").ap()
    dCOSL = nc.dram_tensor("coskl", [HD, KT], F32, kind="ExternalInput").ap()
    dSINL = nc.dram_tensor("sinkl", [HD, KT], F32, kind="ExternalInput").ap()
    dCOSQ = nc.dram_tensor("cosq", [HD, TOK], F32, kind="ExternalInput").ap()
    dSINQ = nc.dram_tensor("sinq", [HD, TOK], F32, kind="ExternalInput").ap()
    dCOSQL = nc.dram_tensor("cosql", [HD, TOK], F32, kind="ExternalInput").ap()
    dSINQL = nc.dram_tensor("sinql", [HD, TOK], F32, kind="ExternalInput").ap()
    dRELP = nc.dram_tensor("relp1", [128, BAND], F32, kind="ExternalInput").ap()
    dMREL = nc.dram_tensor("mrel0", [128, BAND], F32, kind="ExternalInput").ap()
    dBND0 = nc.dram_tensor("bnd0", [128, BAND], F32, kind="ExternalInput").ap()
    dBND1 = nc.dram_tensor("bnd1", [128, BAND], F32, kind="ExternalInput").ap()
    dOUT = nc.dram_tensor("out", [TOK, D], F32, kind="ExternalOutput").ap()

    dSGG = nc.dram_tensor("sgg", [NHC, NDC * 128 * 128], BF16, kind="Internal").ap()
    dSGU = nc.dram_tensor("sgu", [NHC, NDC * 128 * 128], BF16, kind="Internal").ap()
    dSGD = nc.dram_tensor("sgd", [16, NHC * 128 * 128], BF16, kind="Internal").ap()

    identity_bf = nc.inline_tensor(
        np.eye(128, dtype=np.float32).astype(ml_dtypes.bfloat16), name="ident_c").ap()
    ones_row_bf = nc.inline_tensor(
        np.ones((1, 128), dtype=np.float32).astype(ml_dtypes.bfloat16), name="onesr_c").ap()
    ones_col_bf = nc.inline_tensor(
        np.ones((128, 1), dtype=np.float32).astype(ml_dtypes.bfloat16), name="onesc_c").ap()

    dbg = {}
    for nm in DBG:
        shp, dt_ = {
            "d_w": ([1, 16], F32), "d_colsum": ([1, D], F32),
            "d_ln1t": ([128, NDC * KT], BF16), "d_attnout": ([128, 16384], BF16),
            "d_x2": ([TOK, D], F32), "d_xqt": ([128, NDC * THALF], BF16),
            "d_invs": ([128, 8], F32),
        }[nm]
        dbg[nm] = nc.dram_tensor(nm, shp, dt_, kind="ExternalOutput").ap()

    SB0 = 16512

    def A(name, shape, dt_, offset):
        return nc.alloc_sbuf_tensor_at(name, shape, dt_, offset=offset + SB0)
    ln1T = A("ln1T", [128, NDC * KT], BF16, offset=0)
    cosk = A("cosk", [128, KT], BF16, offset=40960)
    sink = A("sink", [128, KT], BF16, offset=43520)
    coskl = A("coskl", [128, KT], BF16, offset=46080)
    sinkl = A("sinkl", [128, KT], BF16, offset=48640)
    cosq = A("cosq", [128, TOK], BF16, offset=51200)
    sinq = A("sinq", [128, TOK], BF16, offset=53248)
    cosql = A("cosql", [128, TOK], BF16, offset=55296)
    sinql = A("sinql", [128, TOK], BF16, offset=57344)
    x2res = A("x2res", [128, NT * D], F32, offset=0)
    wc1b = A("wc1b", [128, 4 * D], BF16, offset=65536)
    xmrow = A("xmrow", [1, D], F32, offset=65536 + 16384)
    xmrow_b = A("xmrow_b", [1, D], BF16, offset=65536 + 24576)
    xmT = A("xmT", [128, 16], BF16, offset=65536 + 28672)
    hTb = A("hTb", [128, 4], BF16, offset=65536 + 28704)
    off = 65536
    qbb = A("qbb", [128, TOK], BF16, offset=off); off += 2048
    qsh = A("qsh", [128, TOK], BF16, offset=off); off += 2048
    qrp = A("qrp", [128, TOK], BF16, offset=off); off += 2048
    kbb = A("kbb", [128, KT], BF16, offset=off); off += 2560
    ksh = A("ksh", [128, KT], BF16, offset=off); off += 2560
    krp = A("krp", [128, KT], BF16, offset=off); off += 2560
    vbf = A("vbf", [128, KT], BF16, offset=off); off += 2560
    vtb = A("vtb", [128, KT], BF16, offset=off); off += 2560
    qtm = A("qtm", [128, KT], BF16, offset=off); off += 2560
    pe = A("pe", [128, 8 * BAND], BF16, offset=off); off += 6144
    ptb = A("ptb", [128, 8 * BAND], BF16, offset=off); off += 6144
    wsl = []
    for i in range(2):
        trio = []
        for nmW in ("wqs", "wks", "wvs"):
            trio.append(A(f"{nmW}{i}", [128, D], BF16, offset=off)); off += 4096
        wsl.append(trio)
    assert off <= 131072, off
    opj = A("opj", [128, 16 * TOK], BF16, offset=65536)
    pprim = A("pprim", [128, NHC * THALF], BF16, offset=65536)
    attnout = A("attnout", [128, HEADS * TOK], BF16, offset=131072)
    xqT = A("xqT", [128, NDC * THALF], BF16, offset=131072)
    cst1 = A("cst1", [128, D], F32, offset=131072)
    ffnout = A("ffnout", [128, 16 * THALF], BF16, offset=131072 + 16384)
    sqscrC1 = A("sqscrC1", [128, D], BF16, offset=131072 + 16384)
    r3 = 163840
    xs0 = A("xs0", [128, D], BF16, offset=r3)
    xs1 = A("xs1", [128, D], BF16, offset=r3 + 4096)
    xsq = A("xsq", [128, D], BF16, offset=r3 + 8192)
    relp = A("relp", [128, BAND], F32, offset=r3 + 12288)
    mrel = A("mrel", [128, BAND], F32, offset=r3 + 13824)
    bnd0 = A("bnd0s", [128, BAND], F32, offset=r3 + 15360)
    bnd1 = A("bnd1s", [128, BAND], F32, offset=r3 + 16896)
    wmf = A("wmf", [128, BAND], F32, offset=r3 + 18432)
    s1buf = A("s1buf", [128, 64], F32, offset=r3 + 20480)
    s2buf = A("s2buf", [128, 64], F32, offset=r3 + 20736)
    tmp64a = A("tmp64a", [128, 64], F32, offset=r3 + 20992)
    tmp64b = A("tmp64b", [128, 64], F32, offset=r3 + 21248)
    irbbuf = A("irbbuf", [128, 64], BF16, offset=r3 + 21504)
    sqscrA = A("sqscrA", [128, D], BF16, offset=r3 + 20480)
    sqscrB = A("sqscrB", [128, D], BF16, offset=r3 + 16384)
    sqscrC5 = A("sqscrC5", [128, D], BF16, offset=r3 + 16384)
    xin0 = A("xin0", [128, D], F32, offset=r3)
    xin1 = A("xin1", [128, D], F32, offset=r3 + 8192)
    ln1st = A("ln1st", [128, D], BF16, offset=r3 + 16384)
    bstage = A("bstage", [128, D], F32, offset=r3 + 20480)
    wgl = [A(f"wgl{i}", [128, D], BF16, offset=r3 + 4096 * i) for i in range(2)]
    wul = [A(f"wul{i}", [128, D], BF16, offset=r3 + 8192 + 4096 * i) for i in range(2)]
    wdl = [A(f"wdl{i}", [128, D], BF16, offset=r3 + 16384 + 4096 * i) for i in range(2)]
    qtb = A("qtb", [128, 1024], F32, offset=r3 + 16384)
    kbf = A("kbf", [128, D], BF16, offset=r3 + 20480)
    ustg = A("ustg", [128, D], F32, offset=r3 + 16384)
    qu = A("qu", [128, D], F32, offset=r3 + 24576)
    cst0 = A("cst0", [128, D], F32, offset=r3 + 24576)
    qrr = A("qrr", [128, 1024], F32, offset=r3 + 32768)
    gsc = A("gsc", [128, THALF], F32, offset=r3 + 32768)
    silb = A("silb", [128, THALF], BF16, offset=r3 + 34816)
    r4 = 200704
    wm_g = A("wm_g", [128, BAND], BF16, offset=r4)
    wm_0 = A("wm_0", [128, BAND], BF16, offset=r4 + 768)
    wm_1 = A("wm_1", [128, BAND], BF16, offset=r4 + 1536)
    identsb = A("identsb", [128, 128], BF16, offset=r4 + 2304)
    onesrow = A("onesrow", [1, 128], BF16, offset=r4 + 2560)
    onescol = A("onescol", [128, 1], BF16, offset=r4 + 2816)
    sm = A("sm", [128, 48], F32, offset=r4 + 2848)
    invs_cols = A("invs_cols", [128, NT], F32, offset=r4 + 3040)
    invs_colsb = A("invs_colsb", [128, NT], BF16, offset=r4 + 3072)
    rmaxn = A("rmaxn", [128, 8], F32, offset=r4 + 3104)
    rmaxs = A("rmaxs", [128, 8], F32, offset=r4 + 3136)
    rsum = A("rsum", [128, 8], F32, offset=r4 + 3168)
    rinv = A("rinv", [128, 8], F32, offset=r4 + 3200)
    bn6 = A("bn6", [128, 6], F32, offset=r4 + 3232)
    scl1 = A("scl1", [1, 48], F32, offset=r4 + 3264)
    sclb = A("sclb", [1, 8], BF16, offset=r4 + 3456)
    wc2b = A("wc2b", [128, 4], BF16, offset=r4 + 3488)
    invsb = A("invsb", [128, THALF], F32, offset=r4 + 3520)
    srowb = A("srowb", [1, THALF], BF16, offset=r4 + 5568)
    ccol = A("ccol", [128, 8], F32, offset=r4 + 6592)
    assert r4 + 6592 + 32 <= 212000

    def smc(i):
        return sm[:, i:i + 1]

    def sc(i):
        return scl1[0:1, i:i + 1]

    PS = nc.alloc_psum_tensor("ps", [128, 4096], F32)

    sems = {k: nc.alloc_semaphore("sem_" + k) for k in ("t", "v", "se")}
    cnt = {e: 0 for e in ("t", "v", "se")}

    def sched():
        return Sched(nc, sems, cnt)

    def cc(i):
        return ccol[:, i:i + 1]

    stats_chain = {}

    def stats_steps(s_, src, scratch, deps):
        key = id(s_)
        prev = stats_chain.get(key)
        deps1 = list(deps) + ([prev[0]] if prev else [])
        st1 = s_.step("v", deps1, (lambda src=src: lambda v:
                      v.reduce_sum(smc(20), src, AX))(), guard=True)
        deps2 = list(deps) + [st1] + ([prev[1]] if prev else [])
        st2 = s_.step("se", deps2, (lambda src=src, scratch=scratch: lambda se:
                      se.activation(scratch, src, AF.Square,
                                    accum_out=smc(21)))(), guard=True)
        stats_chain[key] = (st1, st2)
        sm_ = s_.step("v", [st1], lambda v: v.tensor_scalar(
            smc(4), smc(20), 1.0 / D, None, ALU.mult), guard=True)
        se2 = s_.step("v", [st2], lambda v: v.tensor_scalar(
            smc(22), smc(21), 1.0 / D, None, ALU.mult), guard=True)
        sq_ = s_.step("v", [sm_], lambda v: v.scalar_tensor_tensor(
            smc(23), smc(4), -1.0, smc(4), ALU.mult, ALU.mult), guard=True)
        sv_ = s_.step("v", [se2, sq_], lambda v: v.tensor_add(
            smc(6), smc(22), smc(23)), guard=True)
        return sm_, sv_

    def cc1(i):
        return ccol[0:1, i:i + 1]

    s_ = sched()

    def f_consts(v):
        v.memset(scl1[:], 0.0)
        v.memset(invs_cols[:], 1.0)
        v.memset(ccol[:, 0:1], EPS1)
        v.memset(ccol[:, 1:2], EPS2)
        v.memset(ccol[:, 2:3], -0.5)
        v.memset(ccol[:, 3:4], -5.0)
        v.memset(ccol[:, 4:5], 10.0)
        return v.memset(ccol[:, 5:6], SCALE_ATTN)
    s_.step("v", [], f_consts, guard=True)
    ve_tile = [None] * 64
    for i in range(64):
        xs = xs0 if i % 2 == 0 else xs1
        deps = [ve_tile[i - 2]] if i >= 2 else []
        ld = s_.step("g", deps, (lambda xs=xs, i=i: lambda g:
                     g.dma_start(out=xs[:], in_=dXF[128 * i:128 * i + 128, :]))(),
                     sem=f"xs{i % 2}")

        def mk_ve(xs=xs, i=i):
            def f(v):
                v.reduce_sum(s1buf[:, i:i + 1], xs[:], AX)
                return v.scalar_tensor_tensor(xsq[:], xs[:], 1.0, xs[:],
                                              ALU.mult, ALU.mult,
                                              accum_out=s2buf[:, i:i + 1])
            return f
        vdeps = [ld] + ([ve_tile[i - 1]] if i >= 1 else [])
        ve_tile[i] = s_.step("v", vdeps, mk_ve(), guard=True)
    s_.step("g", [], lambda g: g.dma_start(out=cosk[:], in_=dCOS), sem="cg")
    s_.step("g", [], lambda g: g.dma_start(out=sink[:], in_=dSIN), sem="cg")
    s_.step("g", [], lambda g: g.dma_start(out=coskl[:], in_=dCOSL), sem="cg")
    s_.step("g", [], lambda g: g.dma_start(out=sinkl[:], in_=dSINL), sem="cg")
    s_.step("g", [], lambda g: g.dma_start(out=cosq[:], in_=dCOSQ), sem="cg")
    s_.step("g", [], lambda g: g.dma_start(out=sinq[:], in_=dSINQ), sem="cg")
    s_.step("g", [], lambda g: g.dma_start(out=cosql[:], in_=dCOSQL), sem="cg")
    s_.step("g", [], lambda g: g.dma_start(out=sinql[:], in_=dSINQL), sem="cg")
    s_.step("g", [], lambda g: g.dma_start(
        out=wc1b[:].rearrange("p (h d b) -> p h d b", h=4, d=16),
        in_=dWC1.rearrange("h d a b -> a h d b")), sem="cg")
    s_.step("g", [], lambda g: g.dma_start(
        out=wc2b[:].rearrange("p (h b) -> p h b", h=4),
        in_=dWC2.rearrange("h a b -> a h b"),
        allow_slow_non_contiguous=True), sem="cg")
    s_.step("g", [], lambda g: g.dma_start(out=identsb[:], in_=identity_bf), sem="cg")
    s_.step("g", [], lambda g: g.dma_start(out=onesrow[:], in_=ones_row_bf), sem="cg")
    s_.step("g", [], lambda g: g.dma_start(out=onescol[:], in_=ones_col_bf), sem="cg")
    for dsrc, dst in ((dRELP, relp), (dMREL, mrel), (dBND0, bnd0), (dBND1, bnd1)):
        s_.step("s", [], (lambda dsrc=dsrc, dst=dst: lambda s:
                s.dma_start(out=dst[:], in_=dsrc))(), sem="cs")
    s_.emit()
    if KPH <= 1:
        return nc
    CG_TOTAL = cnt["cg"]
    CS_TOTAL = cnt["cs"]

    s_ = sched()

    def f_cwait(eng):
        eng.wait_ge(sems["cg"], CG_TOTAL)
        eng.wait_ge(sems["cs"], CS_TOTAL)
        return None
    for _e in ("t", "v", "se"):
        s_.step(_e, [], f_cwait, noinc=True)
    a_m = s_.step("v", [], lambda v: v.tensor_scalar(
        s1buf[:], s1buf[:], 1.0 / D, None, ALU.mult), guard=True)
    a_e2 = s_.step("v", [a_m], lambda v: v.tensor_scalar(
        s2buf[:], s2buf[:], 1.0 / D, None, ALU.mult), guard=True)
    a_msq = s_.step("v", [a_m], lambda v: v.scalar_tensor_tensor(
        tmp64a[:], s1buf[:], -1.0, s1buf[:], ALU.mult, ALU.mult), guard=True)
    a_var = s_.step("v", [a_e2, a_msq], lambda v: v.tensor_add(
        s2buf[:], s2buf[:], tmp64a[:]), guard=True)
    a_ln = s_.step("se", [a_var], lambda se: se.activation(
        tmp64a[:], s2buf[:], AF.Ln, bias=cc(0)), guard=True)
    a_ir = s_.step("se", [a_ln], lambda se: se.activation(
        tmp64a[:], tmp64a[:], AF.Exp, scale=cc(2)), guard=True)
    a_mir = s_.step("v", [a_ir], lambda v: v.tensor_mul(
        tmp64b[:], s1buf[:], tmp64a[:]), guard=True)
    a_v1 = s_.step("v", [a_ir], lambda v: v.tensor_mul(
        s2buf[:], s2buf[:], tmp64a[:]), guard=True)
    a_v2 = s_.step("v", [a_v1], lambda v: v.tensor_mul(
        s2buf[:], s2buf[:], tmp64a[:]), guard=True)
    a_irb = s_.step("v", [a_ir], lambda v: v.tensor_copy(
        irbbuf[:], tmp64a[:]), guard=True)
    s_.emit()
    if KPH <= 2:
        return nc

    s_ = sched()
    te_tile = [None] * 64
    for i in range(64):
        xs = xs0 if i % 2 == 0 else xs1
        deps = [te_tile[i - 2]] if i >= 2 else []
        ld = s_.step("g", deps, (lambda xs=xs, i=i: lambda g:
                     g.dma_start(out=xs[:], in_=dXF[128 * i:128 * i + 128, :]))(),
                     sem=f"xs{i % 2}")

        def mk_te(xs=xs, i=i):
            def f(t):
                last = None
                for seg in range(4):
                    last = t.matmul(PS[0:1, 512 * seg:512 * seg + 512],
                                    irbbuf[:, i:i + 1],
                                    xs[:, 512 * seg:512 * seg + 512],
                                    start=(i == 0), stop=(i == 63))
                return last
            return f
        te_tile[i] = s_.step("t", [ld], mk_te(), guard=True)
    b_mirc = s_.step("v", [], lambda v: v.reduce_sum(smc(0), tmp64b[:], AX), guard=True)
    b_virc = s_.step("v", [], lambda v: v.reduce_sum(smc(1), s2buf[:], AX), guard=True)
    b_mircb = s_.step("v", [b_mirc], lambda v: v.tensor_copy(
        invs_colsb[:, 0:1], smc(0)), guard=True)
    b_vircb = s_.step("v", [b_virc], lambda v: v.tensor_copy(
        invs_colsb[:, 1:2], smc(1)), guard=True)
    s_.step("t", [b_mircb, te_tile[63]], lambda t: t.matmul(
        PS[0:1, 2048:2049], invs_colsb[:, 0:1], onescol[:], start=True, stop=True),
        guard=True)
    s_.step("t", [b_vircb], lambda t: t.matmul(
        PS[0:1, 2052:2053], invs_colsb[:, 1:2], onescol[:], start=True, stop=True),
        guard=True)
    s_.emit()
    if KPH <= 3:
        return nc

    s_ = sched()
    c_mir = s_.step("v", [], lambda v: v.tensor_copy(sc(0), PS[0:1, 2048:2049]),
                    guard=True)
    c_vir = s_.step("v", [], lambda v: v.tensor_copy(sc(1), PS[0:1, 2052:2053]),
                    guard=True)
    c_xm = s_.step("v", [c_mir], lambda v: v.tensor_scalar(
        xmrow[:], PS[0:1, 0:2048], sc(0), 1.0 / S, ALU.subtract, ALU.mult),
        guard=True)
    c_xmb = s_.step("v", [c_xm], lambda v: v.tensor_copy(xmrow_b[:], xmrow[:]),
                    guard=True)
    c_sd = s_.step("v", [c_xm], lambda v: v.reduce_sum(sc(2), xmrow[:], AX),
                   guard=True)
    c_mb = s_.step("v", [c_sd], lambda v: v.tensor_scalar(
        sc(3), sc(2), 1.0 / D, None, ALU.mult), guard=True)
    c_mb2 = s_.step("v", [c_mb], lambda v: v.scalar_tensor_tensor(
        sc(4), sc(3), -NTOT / (NTOT - 1.0), sc(3), ALU.mult, ALU.mult), guard=True)
    c_var = s_.step("v", [c_mb2, c_vir], lambda v: v.scalar_tensor_tensor(
        sc(4), sc(1), D / (NTOT - 1.0), sc(4), ALU.mult, ALU.add), guard=True)
    c_vn = s_.step("se", [c_var], lambda se: se.activation(
        sc(5), sc(4), AF.Sigmoid, bias=cc1(3), scale=cc1(4)), guard=True)
    c_xmt = []
    for d in range(16):
        c_xmt.append(s_.step("t", [c_xmb], (lambda d=d: lambda t: t.matmul(
            PS[:, 2056 + d:2057 + d], xmrow_b[0:1, 128 * d:128 * d + 128],
            onescol[0:1, :], start=True, stop=True))(), guard=True))
    c_xmte = s_.step("v", c_xmt, lambda v: v.tensor_copy(
        xmT[:], PS[:, 2056:2072]), guard=True)
    c_hid = []
    for hc in range(4):
        def mk_h(hc=hc):
            def f(t):
                last = None
                for d in range(16):
                    last = t.matmul(
                        PS[:, 2080 + hc:2081 + hc],
                        wc1b[:, 2048 * hc + 128 * d:2048 * hc + 128 * d + 128],
                        xmT[:, d:d + 1], start=(d == 0), stop=(d == 15))
                return last
            return f
        c_hid.append(s_.step("t", [c_xmte], mk_h(), guard=True))
    c_sil = s_.step("se", c_hid, lambda se: se.activation(
        hTb[:], PS[:, 2080:2084], AF.Silu), guard=True)

    def f_logit(t):
        last = None
        for hc in range(4):
            last = t.matmul(PS[0:1, 2088:2089], hTb[:, hc:hc + 1],
                            wc2b[:, hc:hc + 1], start=(hc == 0), stop=(hc == 3))
        return last
    c_log = s_.step("t", [c_sil], f_logit, guard=True)
    c_lrn = s_.step("se", [c_log], lambda se: se.activation(
        sc(6), PS[0:1, 2088:2089], AF.Sigmoid), guard=True)
    c_w0 = s_.step("v", [c_vn, c_lrn], lambda v: v.tensor_add(
        sc(7), sc(5), sc(6)), guard=True)
    c_w = s_.step("v", [c_w0], lambda v: v.tensor_scalar(
        sc(7), sc(7), 96.0, 64.0, ALU.mult, ALU.add), guard=True)
    c_h1 = s_.step("v", [c_w], lambda v: v.tensor_copy(sclb[0:1, 0:1], sc(7)),
                   guard=True)
    c_l1 = s_.step("v", [c_h1], lambda v: v.tensor_sub(
        sc(8), sc(7), sclb[0:1, 0:1]), guard=True)
    c_h2 = s_.step("v", [c_l1], lambda v: v.tensor_copy(sclb[0:1, 1:2], sc(8)),
                   guard=True)
    c_l2 = s_.step("v", [c_h2], lambda v: v.tensor_sub(
        sc(9), sc(8), sclb[0:1, 1:2]), guard=True)
    c_h3 = s_.step("v", [c_l2], lambda v: v.tensor_copy(sclb[0:1, 2:3], sc(9)),
                   guard=True)

    def f_wb(t):
        last = None
        for i in range(3):
            last = t.matmul(PS[:, 2092:2093], onesrow[:],
                            sclb[0:1, i:i + 1], start=(i == 0), stop=(i == 2))
        return last
    c_wb = s_.step("t", [c_h1, c_h2, c_h3], f_wb, guard=True)
    c_w128 = s_.step("v", [c_wb], lambda v: v.tensor_copy(
        smc(2), PS[:, 2092:2093]), guard=True)

    c_m1 = s_.step("v", [c_w128], lambda v: v.tensor_scalar(
        wmf[:], relp[:], smc(2), None, ALU.subtract), guard=True)
    c_m2 = s_.step("v", [c_m1], lambda v: v.tensor_scalar(
        wmf[:], wmf[:], 0.0, BIGNEG, ALU.is_ge, ALU.mult), guard=True)
    c_m3 = s_.step("v", [c_m2], lambda v: v.tensor_add(
        wmf[:], wmf[:], mrel[:]), guard=True)
    c_mg = s_.step("v", [c_m3], lambda v: v.tensor_copy(wm_g[:], wmf[:]),
                   guard=True)
    c_b0 = s_.step("v", [c_m3], lambda v: v.tensor_add(
        bnd0[:], wmf[:], bnd0[:]), guard=True)
    c_m0 = s_.step("v", [c_b0], lambda v: v.tensor_copy(wm_0[:], bnd0[:]),
                   guard=True)
    c_b1 = s_.step("v", [c_m3], lambda v: v.tensor_add(
        bnd1[:], wmf[:], bnd1[:]), guard=True)
    c_m1b = s_.step("v", [c_b1], lambda v: v.tensor_copy(wm_1[:], bnd1[:]),
                    guard=True)
    if "d_w" in dbg:
        s_.step("s", [c_w, c_var, c_mb, c_vn, c_lrn], lambda s: s.dma_start(
            out=dbg["d_w"], in_=scl1[0:1, 0:16]), sem="dbg")
    if "d_colsum" in dbg:
        s_.step("s", [c_xm], lambda s: s.dma_start(out=dbg["d_colsum"],
                                                   in_=xmrow[:]), sem="dbg")
    s_.emit()
    if KPH <= 4:
        return nc

    s_ = sched()
    apply_of = {}
    prev_evac = None
    for jt in range(10):
        xin = xin0 if jt % 2 == 0 else xin1
        src = dX[128 * jt:128 * jt + 128, :] if jt < 8 else \
            dHAL[128 * (jt - 8):128 * (jt - 8) + 128, :]
        dstcol = 256 + 128 * jt if jt < 8 else 128 * (jt - 8)
        deps = [apply_of[jt - 2]] if jt >= 2 else []
        ld = s_.step("s", deps, (lambda xin=xin, src=src: lambda s:
                     s.dma_start(out=xin[:], in_=src))(), sem=f"xi{jt % 2}")
        sm1, sm6 = stats_steps(s_, xin[:], sqscrA[:], [ld])
        sl1 = s_.step("se", [sm6], lambda se: se.activation(
            smc(7), smc(6), AF.Ln, bias=cc(0)), guard=True)
        sl2 = s_.step("se", [sl1], lambda se: se.activation(
            smc(7), smc(7), AF.Exp, scale=cc(2)), guard=True)
        sm7 = s_.step("v", [sl2, sm1], lambda v: v.scalar_tensor_tensor(
            smc(8), smc(4), -1.0, smc(7), ALU.mult, ALU.mult), guard=True)
        ap = s_.step("se", [sm7], (lambda xin=xin: lambda se:
                     se.activation(ln1st[:], xin[:], AF.Identity,
                                   bias=smc(8), scale=smc(7)))(), guard=True)
        apply_of[jt] = ap

        def f_tr(t):
            last = None
            for d in range(16):
                last = t.matmul(PS[:, 128 * d:128 * d + 128],
                                ln1st[:, 128 * d:128 * d + 128],
                                identsb[:], start=True, stop=True)
            return last
        trdeps = [ap] + ([prev_evac] if prev_evac is not None else [])
        tr = s_.step("t", trdeps, f_tr, guard=True)
        ev = s_.step("v", [tr], (lambda dstcol=dstcol: lambda v: v.tensor_copy(
            ln1T[:].rearrange("p (c k) -> p c k", k=KT)[:, :, dstcol:dstcol + 128],
            PS[:, 0:2048].rearrange("p (c k) -> p c k", k=128)))(), guard=True)
        prev_evac = ev
    if "d_ln1t" in dbg:
        s_.step("s", [prev_evac], lambda s: s.dma_start(out=dbg["d_ln1t"],
                                                        in_=ln1T[:]), sem="dbg")
    s_.emit()
    if KPH <= 5:
        return nc

    s_ = sched()

    def qcol(d):
        return 1280 * d + 256

    def kcol(d):
        return 1280 * d
    prev_head_done = None
    te_v_of = {}
    swapmask = []
    for i in range(16):
        swapmask += [2 * i + 1, 2 * i]
    for h in range(HEADS):
        hb = h % 2
        lddeps = [te_v_of[h - 2]] if h >= 2 else []
        lds = []
        for slot, oc in ((0, h), (1, 16 + h), (2, 32 + h)):
            lds.append(s_.step("g", lddeps, (lambda slot=slot, oc=oc, hb=hb:
                lambda g: g.dma_start(
                    out=wsl[hb][slot][:].rearrange("p (d b) -> p d b", b=128),
                    in_=dWQKV[oc].rearrange("d a b -> a d b")))(),
                sem=f"wsl{hb}{slot}"))

        def mk_qkv(which, hb=hb):
            def f(t):
                last = None
                base = [0, 1024, 2560][which]
                for d in range(16):
                    c0 = qcol(d) if which == 0 else kcol(d)
                    w = wsl[hb][which][:, 128 * d:128 * d + 128]
                    segs = ((0, 512), (512, 1024)) if which == 0 else \
                        ((0, 512), (512, 1024), (1024, 1280))
                    for (a, b) in segs:
                        last = t.matmul(PS[:, base + a:base + b], w,
                                        ln1T[:, c0 + a:c0 + b],
                                        start=(d == 0), stop=(d == 15))
                return last
            return f
        depq = [lds[0]] + ([prev_head_done] if prev_head_done is not None else [])
        te_q = s_.step("t", depq, mk_qkv(0), guard=True)
        te_k = s_.step("t", [lds[1]], mk_qkv(1), guard=True)
        te_v = s_.step("t", [lds[2]], mk_qkv(2), guard=True)
        te_v_of[h] = te_v
        ev_q = s_.step("v", [te_q], lambda v: v.tensor_copy(qbb[:], PS[:, 0:1024]),
                       guard=True)
        ev_k = s_.step("v", [te_k], lambda v: v.tensor_copy(kbb[:], PS[:, 1024:2304]),
                       guard=True)
        ev_v = s_.step("v", [te_v], lambda v: v.tensor_copy(vbf[:], PS[:, 2560:3840]),
                       guard=True)

        rq1 = s_.step("v", [ev_q], lambda v: v.stream_shuffle(
            qsh[:], qbb[:], swapmask), guard=True)
        rq2 = s_.step("v", [ev_q], lambda v: v.tensor_mul(
            qrp[:], qbb[:], cosq[:]), guard=True)
        rq2b = s_.step("v", [ev_q], lambda v: v.tensor_mul(
            qtm[:, 0:1024], qbb[:], cosql[:]), guard=True)
        rq2c = s_.step("v", [rq2, rq2b], lambda v: v.tensor_add(
            qrp[:], qrp[:], qtm[:, 0:1024]), guard=True)
        rq3b = s_.step("v", [rq1, rq2c], lambda v: v.tensor_mul(
            qtm[:, 0:1024], qsh[:], sinql[:]), guard=True)
        rq3 = s_.step("v", [rq3b], lambda v: v.tensor_mul(
            qsh[:], qsh[:], sinq[:]), guard=True)
        rq4 = s_.step("v", [rq2c, rq3], lambda v: v.tensor_add(
            qrp[:], qrp[:], qsh[:]), guard=True)
        rope_q = s_.step("v", [rq4, rq3b], lambda v: v.tensor_add(
            qrp[:], qrp[:], qtm[:, 0:1024]), guard=True)

        rk1 = s_.step("v", [ev_k], lambda v: v.stream_shuffle(
            ksh[:], kbb[:], swapmask), guard=True)
        rk2 = s_.step("v", [ev_k], lambda v: v.tensor_mul(
            krp[:], kbb[:], cosk[:]), guard=True)
        rk2b = s_.step("v", [ev_k, rope_q], lambda v: v.tensor_mul(
            qtm[:], kbb[:], coskl[:]), guard=True)
        rk2c = s_.step("v", [rk2, rk2b], lambda v: v.tensor_add(
            krp[:], krp[:], qtm[:]), guard=True)
        rk3b = s_.step("v", [rk1, rk2c], lambda v: v.tensor_mul(
            qtm[:], ksh[:], sinkl[:]), guard=True)
        rk3 = s_.step("v", [rk3b], lambda v: v.tensor_mul(
            ksh[:], ksh[:], sink[:]), guard=True)
        rk4 = s_.step("v", [rk2c, rk3], lambda v: v.tensor_add(
            krp[:], krp[:], ksh[:]), guard=True)
        rope_k = s_.step("v", [rk4, rk3b], lambda v: v.tensor_add(
            krp[:], krp[:], qtm[:]), guard=True)

        def f_vt(t):
            last = None
            for b in range(10):
                last = t.matmul(PS[:, 128 * b:128 * b + 128],
                                vbf[:, 128 * b:128 * b + 128], identsb[:],
                                start=True, stop=True)
            return last
        te_vt = s_.step("t", [ev_q, ev_k, ev_v], f_vt, guard=True)
        ev_vt = s_.step("v", [te_vt], lambda v: v.tensor_copy(
            vtb[:], PS[:, 0:1280]), guard=True)

        def f_sc(t):
            last = None
            for j in range(8):
                t.matmul(PS[:, 512 * j:512 * j + 384],
                         qrp[:, 128 * j:128 * j + 128],
                         krp[:, 128 * j:128 * j + 384],
                         start=True, stop=False)
                wm = wm_0 if j == 0 else (wm_1 if j == 1 else wm_g)
                last = t.matmul(PS[:, 512 * j:512 * j + 384], identsb[:],
                                wm[:], start=False, stop=True)
            return last
        te_sc = s_.step("t", [rope_q, rope_k, ev_vt, ev_v], f_sc, guard=True)
        ve_rm = s_.step("v", [te_sc], lambda v: v.reduce_max(
            rmaxn[:].rearrange("p (s o) -> p s o", o=1),
            PS[:].rearrange("p (s k) -> p s k", k=512)[:, :, 0:384],
            AX, negate=True), guard=True)
        def f_exp(se):
            last = None
            for j in range(8):
                last = se.activation(pe[:, BAND * j:BAND * j + BAND],
                                     PS[:, 512 * j:512 * j + 384], AF.Exp,
                                     bias=rmaxn[:, j:j + 1], scale=1.0,
                                     accum_out=rsum[:, j:j + 1])
            return last
        se_exp = s_.step("se", [ve_rm], f_exp, guard=True)
        ve_ri = s_.step("v", [se_exp], lambda v: v.reciprocal(rinv[:], rsum[:]),
                        guard=True)

        def f_pn(v):
            last = None
            for j in range(8):
                last = v.tensor_scalar(pe[:, BAND * j:BAND * j + BAND],
                                       pe[:, BAND * j:BAND * j + BAND],
                                       rinv[:, j:j + 1], None, ALU.mult)
            return last
        ve_pn = s_.step("v", [ve_ri], f_pn, guard=True)

        def f_pt(t):
            last = None
            for j in range(8):
                for kk in range(3):
                    m = 3 * j + kk
                    last = t.matmul(PS[:, 128 * m:128 * m + 128],
                                    pe[:, BAND * j + 128 * kk:BAND * j + 128 * kk + 128],
                                    identsb[:], start=True, stop=True)
            return last
        te_pt = s_.step("t", [ve_pn, se_exp], f_pt, guard=True)
        ve_pte = s_.step("v", [te_pt], lambda v: v.tensor_copy(
            ptb[:], PS[:, 0:3072]), guard=True)

        def f_ao(t):
            last = None
            for j in range(8):
                for kk in range(3):
                    m = 3 * j + kk
                    last = t.matmul(PS[:, 3072 + 128 * j:3072 + 128 * j + 128],
                                    vtb[:, 128 * (j + kk):128 * (j + kk) + 128],
                                    ptb[:, 128 * m:128 * m + 128],
                                    start=(kk == 0), stop=(kk == 2))
            return last
        te_ao = s_.step("t", [ve_pte], f_ao, guard=True)
        ve_aoe = s_.step("v", [te_ao], (lambda h=h: lambda v: v.tensor_copy(
            attnout[:, 1024 * h:1024 * h + 1024], PS[:, 3072:4096]))(), guard=True)
        prev_head_done = ve_aoe
    if "d_attnout" in dbg:
        s_.step("s", [prev_head_done], lambda s: s.dma_start(
            out=dbg["d_attnout"], in_=attnout[:]), sem="dbg")
    s_.emit()
    if KPH <= 6:
        return nc

    s_ = sched()
    ev_of = {}
    prev_ev = None
    for o in range(16):
        lddeps = [ev_of[o - 2]] if o >= 2 else []
        ld = s_.step("g", lddeps, (lambda o=o: lambda g: g.dma_start(
            out=wsl[1][o % 2][:].rearrange("p (h b) -> p h b", b=128),
            in_=dWOUT[o].rearrange("h a b -> a h b")))(), sem=f"wo{o % 2}")

        def f_opj(t, o=o):
            last = None
            for hh in range(16):
                w = wsl[1][o % 2][:, 128 * hh:128 * hh + 128]
                for (a, b) in ((0, 512), (512, 1024)):
                    last = t.matmul(PS[:, a:b], w,
                                    attnout[:, 1024 * hh + a:1024 * hh + b],
                                    start=(hh == 0), stop=(hh == 15))
            return last
        dep = [ld] + ([prev_ev] if prev_ev is not None else [])
        te = s_.step("t", dep, f_opj, guard=True)
        ev = s_.step("v", [te], (lambda o=o: lambda v: v.tensor_copy(
            opj[:, 1024 * o:1024 * o + 1024], PS[:, 0:1024]))(), guard=True)
        prev_ev = ev
        ev_of[o] = ev
    opj_done = prev_ev
    resid_of = {}
    prev_apply = None
    for jt in (range(8) if KPH != 61 else []):
        xin = xin0 if jt % 2 == 0 else xin1
        lddeps = [resid_of[jt - 2]] if jt >= 2 else []
        ld = s_.step("s", lddeps, (lambda xin=xin, jt=jt: lambda s: s.dma_start(
            out=xin[:], in_=dX[128 * jt:128 * jt + 128, :]))(), sem=f"xi{jt % 2}")

        def f_tr(t, jt=jt):
            last = None
            for o in range(16):
                last = t.matmul(PS[:, 2048 + 128 * o:2048 + 128 * o + 128],
                                opj[:, 1024 * o + 128 * jt:1024 * o + 128 * jt + 128],
                                identsb[:], start=True, stop=True)
            return last
        trdeps = [opj_done] + ([prev_apply] if prev_apply is not None else [])
        tr = s_.step("t", trdeps, f_tr, guard=True)
        if KPH == 62:
            prev_apply = tr
            resid_of[jt] = tr
            continue
        if KPH == 631:
            u6 = s_.step("v", [tr], lambda v: v.reduce_sum(
                smc(20), PS[:, 2048:4096], AX), guard=True)
            prev_apply = u6
            resid_of[jt] = u6
            continue
        if KPH == 632:
            u6 = s_.step("se", [tr], lambda se: se.activation(
                sqscrB[:], PS[:, 2048:4096], AF.Square, accum_out=smc(21)),
                guard=True)
            prev_apply = u6
            resid_of[jt] = u6
            continue
        u1, u6 = stats_steps(s_, PS[:, 2048:4096], sqscrB[:], [tr])
        if KPH == 63:
            prev_apply = u6
            resid_of[jt] = u6
            continue
        w1 = s_.step("se", [u6], lambda se: se.activation(
            smc(7), smc(6), AF.Ln, bias=cc(0)), guard=True)
        w2 = s_.step("se", [w1], lambda se: se.activation(
            smc(7), smc(7), AF.Exp, scale=cc(2)), guard=True)
        u7 = s_.step("v", [w2, u1], lambda v: v.scalar_tensor_tensor(
            smc(8), smc(4), -1.0, smc(7), ALU.mult, ALU.mult), guard=True)
        ap = s_.step("se", [u7], lambda se: se.activation(
            bstage[:], PS[:, 2048:4096], AF.Identity, bias=smc(8), scale=smc(7)),
            guard=True)
        prev_apply = ap
        if KPH == 64:
            resid_of[jt] = ap
            continue
        rs = s_.step("v", [ap, ld], (lambda xin=xin, jt=jt: lambda v:
            v.tensor_add(x2res[:, 2048 * jt:2048 * jt + 2048], xin[:], bstage[:]))(),
            guard=True)
        resid_of[jt] = rs
    if "d_x2" in dbg:
        for jt in range(8):
            s_.step("s", [resid_of[7]], (lambda jt=jt: lambda s: s.dma_start(
                out=dbg["d_x2"][128 * jt:128 * jt + 128, :],
                in_=x2res[:, 2048 * jt:2048 * jt + 2048]))(), sem="dbg")
    s_.emit()
    if KPH <= 7 or KPH in (61, 62, 63, 64, 631, 632):
        return nc

    sd_half0_mark = None
    for H in range(2):
        s_ = sched()
        prev_trk_ev = None
        xqt_done = []
        kdone_all = []
        for lt in range(NT_H):
            tj = NT_H * H + lt
            x2t = x2res[:, 2048 * tj:2048 * tj + 2048]
            sdeps = [xqt_done[-1]] if xqt_done else []
            q1, q6 = stats_steps(s_, x2t, sqscrC1[:], sdeps)
            bn = q1
            q7 = s_.step("se", [q6], lambda se: se.activation(
                smc(7), smc(6), AF.Ln, bias=cc(0)), guard=True)
            q8 = s_.step("se", [q7], lambda se: se.activation(
                smc(7), smc(7), AF.Exp, scale=cc(2)), guard=True)
            q9 = s_.step("v", [q8], lambda v: v.tensor_mul(
                smc(9), smc(6), smc(7)), guard=True)
            q10 = s_.step("v", [q9], lambda v: v.tensor_mul(
                smc(9), smc(9), smc(7)), guard=True)
            q11 = s_.step("se", [q10], lambda se: se.activation(
                smc(10), smc(9), AF.Ln, bias=cc(1)), guard=True)
            q12 = s_.step("se", [q11], lambda se: se.activation(
                smc(10), smc(10), AF.Exp, scale=cc(2)), guard=True)
            q13 = s_.step("v", [q12, q8], lambda v: v.tensor_mul(
                smc(11), smc(7), smc(10)), guard=True)
            qmx = s_.step("v", [bn], (lambda x2t=x2t: lambda v:
                          v.reduce_max(smc(12), x2t, AX))(), guard=True)
            qmn = s_.step("v", [qmx], (lambda x2t=x2t: lambda v:
                          v.tensor_reduce(smc(13), x2t, AX, ALU.min))(), guard=True)
            qa1 = s_.step("v", [qmx, q1], lambda v: v.tensor_sub(
                smc(14), smc(12), smc(4)), guard=True)
            qa2 = s_.step("v", [qmn, q1], lambda v: v.tensor_sub(
                smc(15), smc(4), smc(13)), guard=True)
            qam = s_.step("v", [qa1, qa2], lambda v: v.tensor_max(
                smc(14), smc(14), smc(15)), guard=True)
            qab = s_.step("v", [qam, q13], lambda v: v.tensor_mul(
                smc(16), smc(14), smc(11)), guard=True)
            qrc = s_.step("v", [qab], lambda v: v.reciprocal(
                smc(17), smc(16)), guard=True)
            qsx = s_.step("v", [qrc], lambda v: v.tensor_scalar(
                smc(17), smc(17), 127.0, None, ALU.mult), guard=True)
            qsy = s_.step("v", [qsx, q13], lambda v: v.tensor_mul(
                smc(18), smc(17), smc(11)), guard=True)
            qbi = s_.step("v", [qsy, q1], lambda v: v.scalar_tensor_tensor(
                smc(19), smc(4), -1.0, smc(18), ALU.mult, ALU.mult), guard=True)
            qb2 = s_.step("v", [qbi], lambda v: v.tensor_scalar(
                smc(19), smc(19), 128.0, None, ALU.add), guard=True)
            qiv = s_.step("v", [qab], (lambda tj=tj: lambda v: v.tensor_scalar(
                invs_cols[:, tj:tj + 1], smc(16), 1.0 / 127.0, None, ALU.mult))(),
                guard=True)
            kdone = []
            for hh in range(2):
                xsl = x2res[:, 2048 * tj + 1024 * hh:2048 * tj + 1024 * hh + 1024]
                up = s_.step("se", [qb2], (lambda xsl=xsl, hh=hh: lambda se:
                    se.activation(qu[:, 1024 * hh:1024 * hh + 1024], xsl, AF.Identity,
                                  bias=smc(19), scale=smc(18)))(), guard=True)

                uap = qu[:, 1024 * hh:1024 * hh + 1024]
                cdeps = [up] + (kdone_all[-1:] if kdone_all else [])
                cprev = s_.step("v", cdeps, (lambda uap=uap: lambda v:
                    v.tensor_copy(qrr[:], uap))(), guard=True)
                for b in (128, 64, 32, 16, 8, 4, 2, 1):
                    c_ts = s_.step("v", [cprev], (lambda b=b: lambda v:
                        v.tensor_scalar(qtb[:], qrr[:], float(b) - 0.5, float(b),
                                        ALU.is_ge, ALU.mult))(), guard=True)
                    cprev = s_.step("v", [c_ts], lambda v:
                        v.tensor_sub(qrr[:], qrr[:], qtb[:]), guard=True)
                cprev = s_.step("v", [cprev], (lambda uap=uap: lambda v:
                    v.tensor_sub(qrr[:], uap, qrr[:]))(), guard=True)
                kd = s_.step("v", [cprev], (lambda hh=hh: lambda v:
                    v.tensor_scalar(kbf[:, 1024 * hh:1024 * hh + 1024],
                                    qrr[:], 128.0, None, ALU.subtract))(),
                    guard=True)
                kdone.append(kd)
                kdone_all.append(kd)

            def f_trk(t):
                last = None
                for d in range(16):
                    last = t.matmul(PS[:, 128 * d:128 * d + 128],
                                    kbf[:, 128 * d:128 * d + 128],
                                    identsb[:], start=True, stop=True)
                return last
            trdeps = list(kdone) + ([prev_trk_ev] if prev_trk_ev is not None else [])
            trk = s_.step("t", trdeps, f_trk, guard=True)
            evk = s_.step("v", [trk], (lambda lt=lt: lambda v: v.tensor_copy(
                xqT[:].rearrange("p (c k) -> p c k", k=THALF)[:, :, 128 * lt:128 * lt + 128],
                PS[:, 0:2048].rearrange("p (c k) -> p c k", k=128)))(), guard=True)
            prev_trk_ev = evk
            xqt_done.append(evk)
        quant_end = xqt_done[-1]
        iv0 = s_.step("v", [quant_end], (lambda H=H: lambda v: v.tensor_copy(
            invs_colsb[:, NT_H * H:NT_H * H + NT_H],
            invs_cols[:, NT_H * H:NT_H * H + NT_H]))(), guard=True)
        ivt = []
        for lt in range(NT_H):
            tj = NT_H * H + lt
            ivt.append(s_.step("t", [iv0], (lambda tj=tj, lt=lt: lambda t: t.matmul(
                PS[0:1, 3072 + 128 * lt:3072 + 128 * lt + 128],
                invs_colsb[:, tj:tj + 1], identsb[:], start=True, stop=True))(),
                guard=True))
        iv1 = s_.step("v", ivt, lambda v: v.tensor_copy(
            srowb[0:1, :], PS[0:1, 3072:3584]), guard=True)
        ivb = s_.step("t", [iv1], lambda t: t.matmul(
            PS[:, 3584:4096], onesrow[:], srowb[0:1, :], start=True, stop=True),
            guard=True)
        iv2 = s_.step("v", [ivb], lambda v: v.tensor_copy(
            invsb[:], PS[:, 3584:4096]), guard=True)
        local_dbg = []
        if H == 0 and "d_xqt" in dbg:
            local_dbg.append(s_.step("s", [quant_end], lambda s: s.dma_start(
                out=dbg["d_xqt"], in_=xqT[:]), sem="dbg"))
        if H == 0 and "d_invs" in dbg:
            local_dbg.append(s_.step("s", [iv2], lambda s: s.dma_start(
                out=dbg["d_invs"], in_=invs_cols[:]), sem="dbg"))

        te_gu_of = {}
        pp_of = {}
        g2_of = {}
        gust_of = {}
        prev_sgn = None
        for hc in range(NHC):
            hb = hc % 2
            lddeps = [te_gu_of[hc - 2]] if hc >= 2 else [quant_end]
            if H == 0 and hc >= 2:
                lddeps = lddeps + gust_of[hc - 2]
            if H == 0:
                ldg_ = s_.step("g", lddeps, (lambda hc=hc, hb=hb: lambda g:
                    g.dma_start(out=wgl[hb][:].rearrange("p (d b) -> p d b", b=128),
                                in_=dWG[hc].rearrange("d a b -> a d b")))(),
                    sem=f"wg{hb}")
                ldu_ = s_.step("s", [prev_sgn] if prev_sgn is not None
                               else [quant_end],
                    (lambda hc=hc: lambda s:
                    s.dma_start(out=ustg[:].rearrange("p (d b) -> p d b", b=128),
                                in_=dWU[hc].rearrange("d a b -> a d b")))(),
                    sem="wuf")
                sgn = s_.step("se", [ldg_, ldu_], (lambda hb=hb: lambda se:
                    (se.activation(wgl[hb][:], wgl[hb][:], AF.Sign),
                     se.activation(wul[hb][:], ustg[:], AF.Sign))[-1])(),
                    guard=True)
                prev_sgn = sgn
                stg1 = s_.step("s", [sgn], (lambda hc=hc, hb=hb: lambda s:
                    s.dma_start(out=dSGG[hc].rearrange("(a c) -> a c", a=128),
                                in_=wgl[hb][:]))(), sem=f"sgstg{hb}")
                stg2 = s_.step("s", [sgn], (lambda hc=hc, hb=hb: lambda s:
                    s.dma_start(out=dSGU[hc].rearrange("(a c) -> a c", a=128),
                                in_=wul[hb][:]))(), sem=f"sgstu{hb}")
                gust_of[hc] = [stg1, stg2]
                wdep = [sgn]
            else:
                mark = sd_half0_mark

                def mk_ldg(hc=hc, hb=hb, mark=mark):
                    def f(s):
                        if hc == 0 and mark is not None:
                            for kk, vv in mark.items():
                                s.wait_ge(sems[kk], vv)
                        return s.dma_start(
                            out=wgl[hb][:],
                            in_=dSGG[hc].rearrange("(a c) -> a c", a=128))
                    return f
                ldg_ = s_.step("s", lddeps, mk_ldg(), sem=f"wgh{hb}")
                ldu_ = s_.step("s", lddeps, (lambda hc=hc, hb=hb: lambda s:
                    s.dma_start(out=wul[hb][:],
                                in_=dSGU[hc].rearrange("(a c) -> a c", a=128)))(),
                    sem=f"wuh{hb}")
                wdep = [ldg_, ldu_]

            pb = 1024 * (hc % 2)

            def f_gu(t, hb=hb, pb=pb):
                last = None
                for d in range(16):
                    last = t.matmul(PS[:, pb:pb + 512],
                                    wgl[hb][:, 128 * d:128 * d + 128],
                                    xqT[:, 512 * d:512 * d + 512],
                                    start=(d == 0), stop=(d == 15))
                for d in range(16):
                    last = t.matmul(PS[:, pb + 512:pb + 1024],
                                    wul[hb][:, 128 * d:128 * d + 128],
                                    xqT[:, 512 * d:512 * d + 512],
                                    start=(d == 0), stop=(d == 15))
                return last
            gudeps = list(wdep) + [iv2]
            if hc >= 2:
                gudeps.append(pp_of[hc - 2])
            te_gu = s_.step("t", gudeps, f_gu, guard=True)
            te_gu_of[hc] = te_gu
            g1deps = [te_gu] + ([g2_of[hc - 1]] if hc >= 1 else [])
            g1 = s_.step("v", g1deps, (lambda pb=pb: lambda v: v.tensor_mul(
                gsc[:], PS[:, pb:pb + 512], invsb[:]))(), guard=True)
            g2deps = [g1] + ([pp_of[hc - 1]] if hc >= 1 else [])
            g2 = s_.step("se", g2deps, lambda se: se.activation(
                silb[:], gsc[:], AF.Silu), guard=True)
            g2_of[hc] = g2
            pp = s_.step("v", [g2], (lambda hc=hc, pb=pb: lambda v: v.tensor_mul(
                pprim[:, 512 * hc:512 * hc + 512], silb[:],
                PS[:, pb + 512:pb + 1024]))(), guard=True)
            pp_of[hc] = pp
        pp_all = pp_of[NHC - 1]

        ev_wd = {}
        last_q_of = {}
        wdst_of = {}
        for o in range(16):
            for qq in range(4):
                key = o * 4 + qq
                hb = key % 2
                lddeps = [last_q_of[key - 2]] if key >= 2 else [quant_end]
                if H == 0 and key >= 2:
                    lddeps = lddeps + [wdst_of[key - 2]]
                if H == 0 and key < 2 and prev_sgn is not None:
                    lddeps = lddeps + [prev_sgn]
                if H == 0:
                    ldd = s_.step("g", lddeps, (lambda o=o, qq=qq, hb=hb: lambda g:
                        g.dma_start(out=wdl[hb][:].rearrange("p (d b) -> p d b", b=128),
                                    in_=dWD[o, 16 * qq:16 * qq + 16]
                                    .rearrange("d a b -> a d b")))(), sem=f"wd{hb}")
                    sgd = s_.step("se", [ldd], (lambda hb=hb: lambda se:
                        se.activation(wdl[hb][:], wdl[hb][:], AF.Sign))(), guard=True)
                    wdst_of[key] = s_.step("s", [sgd], (lambda o=o, qq=qq, hb=hb:
                        lambda s: s.dma_start(
                            out=dSGD[o, 262144 * qq:262144 * (qq + 1)]
                            .rearrange("(a c) -> a c", a=128),
                            in_=wdl[hb][:]))(), sem=f"sgstd{hb}")
                    wdep2 = [sgd]
                else:
                    ldd = s_.step("s", lddeps, (lambda o=o, qq=qq, hb=hb: lambda s:
                        s.dma_start(out=wdl[hb][:],
                                    in_=dSGD[o, 262144 * qq:262144 * (qq + 1)]
                                    .rearrange("(a c) -> a c", a=128)))(), sem=f"wdh{hb}")
                    wdep2 = [ldd]

                def f_wd(t, o=o, qq=qq, hb=hb):
                    last = None
                    bank = o % 8
                    for dd in range(16):
                        hcg = 16 * qq + dd
                        last = t.matmul(
                            PS[:, 512 * bank:512 * bank + 512],
                            wdl[hb][:, 128 * dd:128 * dd + 128],
                            pprim[:, 512 * hcg:512 * hcg + 512],
                            start=(qq == 0 and dd == 0),
                            stop=(qq == 3 and dd == 15))
                    return last
                wdd = wdep2 + [pp_all]
                if o >= 8 and qq == 0:
                    wdd.append(ev_wd[o - 8])
                tq = s_.step("t", wdd, f_wd, guard=True)
                last_q_of[key] = tq
            ev_wd[o] = s_.step("v", [last_q_of[o * 4 + 3]], (lambda o=o: lambda v:
                v.tensor_copy(ffnout[:, 512 * o:512 * o + 512],
                              PS[:, 512 * (o % 8):512 * (o % 8) + 512]))(),
                guard=True)

        prev_ap2 = None
        store_of = {}
        for lt in range(NT_H):
            tj = NT_H * H + lt
            cstg = cst0 if lt % 2 == 0 else cst1

            def f_trf(t, lt=lt):
                last = None
                for o in range(16):
                    last = t.matmul(
                        PS[:, 128 * o:128 * o + 128],
                        ffnout[:, 512 * o + 128 * lt:512 * o + 128 * lt + 128],
                        identsb[:], start=True, stop=True)
                return last
            trdeps = [ev_wd[o] for o in range(16)]
            if prev_ap2 is not None:
                trdeps.append(prev_ap2)
            if H == 0 and lt == 0:
                trdeps += [wdst_of[62], wdst_of[63]] + local_dbg[-1:]
            trf = s_.step("t", trdeps, f_trf, guard=True)
            e1, e6 = stats_steps(s_, PS[:, 0:2048], sqscrC5[:], [trf])
            e7 = s_.step("v", [e1], (lambda tj=tj: lambda v: v.tensor_mul(
                smc(4), smc(4), invs_cols[:, tj:tj + 1]))(), guard=True)
            e8 = s_.step("v", [e6], (lambda tj=tj: lambda v: v.tensor_mul(
                smc(6), smc(6), invs_cols[:, tj:tj + 1]))(), guard=True)
            e9 = s_.step("v", [e8], (lambda tj=tj: lambda v: v.tensor_mul(
                smc(6), smc(6), invs_cols[:, tj:tj + 1]))(), guard=True)
            f1 = s_.step("se", [e9], lambda se: se.activation(
                smc(7), smc(6), AF.Ln, bias=cc(0)), guard=True)
            f2 = s_.step("se", [f1], lambda se: se.activation(
                smc(7), smc(7), AF.Exp, scale=cc(2)), guard=True)
            e10 = s_.step("v", [f2], (lambda tj=tj: lambda v: v.tensor_mul(
                smc(9), smc(7), invs_cols[:, tj:tj + 1]))(), guard=True)
            e11 = s_.step("v", [e7, f2], lambda v: v.scalar_tensor_tensor(
                smc(8), smc(4), -1.0, smc(7), ALU.mult, ALU.mult), guard=True)
            apdeps = [e10, e11]
            if lt >= 2:
                apdeps.append(store_of[lt - 2])
            ap2 = s_.step("se", apdeps, (lambda cstg=cstg: lambda se:
                se.activation(cstg[:], PS[:, 0:2048], AF.Identity,
                              bias=smc(8), scale=smc(9)))(), guard=True)
            prev_ap2 = ap2
            rs2 = s_.step("v", [ap2], (lambda cstg=cstg, tj=tj: lambda v:
                v.tensor_add(cstg[:], x2res[:, 2048 * tj:2048 * tj + 2048],
                             cstg[:]))(), guard=True)
            st = s_.step("s", [rs2], (lambda cstg=cstg, tj=tj: lambda s:
                s.dma_start(out=dOUT[128 * tj:128 * tj + 128, :], in_=cstg[:]))(),
                sem=f"ost{lt % 2}")
            store_of[lt] = st

        def f_final(s):
            s.wait_ge(sems["ost0"], cnt_snapshot["ost0"])
            s.wait_ge(sems["ost1"], cnt_snapshot["ost1"])
            if "dbg" in sems:
                s.wait_ge(sems["dbg"], cnt_snapshot.get("dbg", 0))
            return None
        cnt_snapshot = {"ost0": cnt.get("ost0", 0) + 16 * NT_H // 2 * 2 // 2,
                        "ost1": cnt.get("ost1", 0) + 16 * NT_H // 2,
                        "dbg": cnt.get("dbg", 0)}
        s_.step("s", [store_of[NT_H - 1], store_of[NT_H - 2]], f_final, noinc=True)
        s_.emit()
        if H == 0 and KPH <= 8:
            return nc
        if H == 0:
            sd_half0_mark = {k: cnt.get(k, 0)
                             for k in ("sgstg0", "sgstg1", "sgstu0", "sgstu1",
                                       "sgstd0", "sgstd1")}

    return nc


def _blocked(w, rows_per=128, cols_per=128):
    R, C = w.shape
    return np.ascontiguousarray(
        w.reshape(R // rows_per, rows_per, C // cols_per, cols_per)
        .transpose(2, 0, 1, 3))


def kernel(L0_attn_x, L0_attn_w_qkv, L0_attn_w_out, L0_attn_w_c1, L0_attn_w_c2,
           L0_ffn_W_g, L0_ffn_W_u, L0_ffn_W_d):
    x = np.ascontiguousarray(np.asarray(L0_attn_x, dtype=np.float32)[0])
    wqkv_s = _blocked(np.ascontiguousarray(np.asarray(L0_attn_w_qkv, np.float32).T))
    wout_s = _blocked(np.ascontiguousarray(np.asarray(L0_attn_w_out, np.float32).T))
    wc1_s = np.ascontiguousarray(
        np.asarray(L0_attn_w_c1, np.float32).T
        .reshape(16, 128, 4, 128).transpose(2, 0, 1, 3))
    wc2_s = np.ascontiguousarray(
        np.asarray(L0_attn_w_c2, np.float32).T.reshape(4, 128, 1))
    wg_s = _blocked(np.ascontiguousarray(np.asarray(L0_ffn_W_g, np.float32).T))
    wu_s = _blocked(np.ascontiguousarray(np.asarray(L0_ffn_W_u, np.float32).T))
    wd_s = _blocked(np.ascontiguousarray(np.asarray(L0_ffn_W_d, np.float32).T))

    import ctypes
    _e = np.arange(0, HD, 2, dtype=np.float32).astype(np.float64) / HD
    _pw = (np.float64(10000.0) ** _e).astype(np.float32)
    inv_freq32 = (1.0 / _pw.astype(np.float64)).astype(np.float32)
    _posg = np.arange(-HALO, S, dtype=np.float32).astype(np.float64)
    freqs32 = (inv_freq32.astype(np.float64)[:, None] * _posg[None, :]
               ).astype(np.float32)
    _libm = ctypes.CDLL("libm.so.6")
    _libm.cosf.restype = ctypes.c_float
    _libm.cosf.argtypes = [ctypes.c_float]
    _libm.sinf.restype = ctypes.c_float
    _libm.sinf.argtypes = [ctypes.c_float]
    _ff = freqs32.ravel()
    cos32g = np.fromiter((_libm.cosf(float(v)) for v in _ff), np.float32,
                         _ff.size).reshape(freqs32.shape)
    sin32g = np.fromiter((_libm.sinf(float(v)) for v in _ff), np.float32,
                         _ff.size).reshape(freqs32.shape)
    cos32g = np.concatenate([cos32g, cos32g], axis=0)
    sin32g = np.concatenate([sin32g, sin32g], axis=0)
    sgn = np.where(np.arange(HD) % 2 == 0, -1.0, 1.0)

    relp1 = (257.0 + np.arange(128)[:, None] - np.arange(BAND)[None, :]).astype(np.float32)
    mrel0 = np.where(relp1 - 1.0 >= 1.0, 0.0, BIGNEG).astype(np.float32)
    mrel0 = np.where(relp1 >= 1.0, 0.0, BIGNEG).astype(np.float32)

    in_maps = []
    for c in range(NCORES):
        start = TOK * c
        cos64 = cos32g[:, start:start + HALO + TOK].astype(np.float64)
        sin64 = sin32g[:, start:start + HALO + TOK].astype(np.float64) * sgn[:, None]
        cosk_ = cos64.astype(np.float32)
        sink_ = sin64.astype(np.float32)
        coskl_ = (cos64 - cosk_).astype(np.float32)
        sinkl_ = (sin64 - sink_).astype(np.float32)
        sc64 = float(HD) ** -0.5
        cq64 = cos64[:, HALO:] * sc64
        sq64 = sin64[:, HALO:] * sc64
        cosq_ = cq64.astype(np.float32)
        sinq_ = sq64.astype(np.float32)
        cosql_ = (cq64 - cosq_).astype(np.float32)
        sinql_ = (sq64 - sinq_).astype(np.float32)
        if c == 0:
            halo = np.zeros((HALO, D), np.float32)
            b0 = np.broadcast_to(
                np.where(np.arange(BAND)[None, :] < 256, BIGNEG, 0.0),
                (128, BAND)).astype(np.float32)
            b1 = np.broadcast_to(
                np.where(np.arange(BAND)[None, :] < 128, BIGNEG, 0.0),
                (128, BAND)).astype(np.float32)
        else:
            halo = np.ascontiguousarray(x[start - HALO:start])
            b0 = np.zeros((128, BAND), np.float32)
            b1 = np.zeros((128, BAND), np.float32)
        in_maps.append({
            "x_own": np.ascontiguousarray(x[start:start + TOK]),
            "x_halo": halo, "x_full": x,
            "wqkv_s": wqkv_s, "wout_s": wout_s, "wc1_s": wc1_s, "wc2_s": wc2_s,
            "wg_s": wg_s, "wu_s": wu_s, "wd_s": wd_s,
            "cosk": cosk_, "sink": sink_, "coskl": coskl_, "sinkl": sinkl_,
            "cosq": cosq_, "sinq": sinq_, "cosql": cosql_, "sinql": sinql_,
            "relp1": relp1, "mrel0": mrel0, "bnd0": b0, "bnd1": b1,
        })

    nc = build_program()
    res = run_bass_kernel_spmd(nc, in_maps, core_ids=list(range(NCORES)))
    out = np.concatenate([np.asarray(res.results[c]["out"]) for c in range(NCORES)],
                         axis=0)
    if DBG:
        kernel.dbg_results = res.results
    return out.reshape(1, S, D).astype(np.float32)
